# version 08
# baseline (speedup 1.0000x reference)
import math
from contextlib import ExitStack

import numpy as np
import ml_dtypes

import concourse.bass as bass
import concourse.mybir as mybir
from concourse.bass_utils import run_bass_kernel_spmd

F32 = mybir.dt.float32
BF16 = mybir.dt.bfloat16
AF = mybir.ActivationFunctionType
ALU = mybir.AluOpType
AXL = mybir.AxisListType
BF = ml_dtypes.bfloat16

B, S, E = 16, 2048, 1024
NCORE = 8
TC = B * S // NCORE
GRP = 512
NG = TC // GRP
NEXP = 8
NE = NEXP + 1
SCALE = 1.0 / math.sqrt(1.0 + 2.0 / 8.0)
EPS = 1e-5
PI = math.pi

_lx = np.log(np.array([2.0, 4.0, 8.0, 16.0, 32.0], dtype=np.float64))
SLX = float(_lx.sum())
DEN = float(5.0 * (_lx * _lx).sum() - SLX * SLX)


class Sem:
    def __init__(self, name):
        self.name = name
        self.h = None


class Em:

    def __init__(self, mode, labels, eng=None):
        self.mode = mode
        self.labels = labels
        self.counters = {}
        self.eng = eng
        self._last = None

    def do(self, fn):
        if self.mode == "emit":
            self._last = fn()

    def inc(self, sem, label=None, amt=1):
        v = self.counters.get(sem.name, 0) + amt
        self.counters[sem.name] = v
        if label is not None:
            key = (sem.name, label)
            if self.mode == "count":
                assert key not in self.labels, f"dup label {key}"
                self.labels[key] = v
        if self.mode == "emit":
            self._last.then_inc(sem.h, amt)

    def alias(self, sem, label):
        v = self.counters.get(sem.name, 0)
        if self.mode == "count":
            key = (sem.name, label)
            assert key not in self.labels, f"dup label {key}"
            self.labels[key] = v

    def wait(self, sem, label, delta=0):
        if self.mode == "emit":
            self.eng.wait_ge(sem.h, self.labels[(sem.name, label)] + delta)

    def wait_val(self, sem, v):
        if self.mode == "emit":
            self.eng.wait_ge(sem.h, v)


def build_program():
    nc = bass.Bass("TRN2", target_bir_lowering=False)

    d_xT = nc.dram_tensor("xT", [E, TC], F32, kind="ExternalInput").ap()
    d_xTrev = nc.dram_tensor("xTrev", [E, TC], F32, kind="ExternalInput").ap()
    d_xfirst = nc.dram_tensor("xfirst", [1, E], F32, kind="ExternalInput").ap()
    d_w1 = nc.dram_tensor("W1TT", [NEXP, 8, 128, 4096], F32, kind="ExternalInput").ap()
    d_w2 = nc.dram_tensor("W2TT", [NEXP, 8, 128, 4096], F32, kind="ExternalInput").ap()
    d_ws1 = nc.dram_tensor("Ws1TT", [8, 128, 4096], F32, kind="ExternalInput").ap()
    d_ws2 = nc.dram_tensor("Ws2TT", [8, 128, 4096], F32, kind="ExternalInput").ap()
    d_wr = nc.dram_tensor("wrT3", [8, 128, 8], F32, kind="ExternalInput").ap()
    d_wel = nc.dram_tensor("WelT3", [8, 128, 24], F32, kind="ExternalInput").ap()
    d_wle = nc.dram_tensor("Wle", [24, E], F32, kind="ExternalInput").ap()
    d_b1 = nc.dram_tensor("b1r", [NEXP, 128, 32], F32, kind="ExternalInput").ap()
    d_bs1 = nc.dram_tensor("bs1r", [128, 32], F32, kind="ExternalInput").ap()
    d_b2 = nc.dram_tensor("b2", [8, E], F32, kind="ExternalInput").ap()
    d_bs2 = nc.dram_tensor("bs2r", [128, 8], F32, kind="ExternalInput").ap()
    d_ble = nc.dram_tensor("bler", [128, 8], F32, kind="ExternalInput").ap()
    d_bel = nc.dram_tensor("bel", [24, 1], F32, kind="ExternalInput").ap()
    d_ht = nc.dram_tensor("HT", [24, 12], BF16, kind="ExternalInput").ap()
    d_ident = nc.dram_tensor("ident", [128, 128], BF16, kind="ExternalInput").ap()
    d_sel = nc.dram_tensor("sel8", [8, 8 * 128], BF16, kind="ExternalInput").ap()
    d_ones = nc.dram_tensor("onesf", [128, 2], F32, kind="ExternalInput").ap()
    d_onesr = nc.dram_tensor("onesrf", [1, 128], F32, kind="ExternalInput").ap()
    d_onesb = nc.dram_tensor("onesrb", [1, 128], BF16, kind="ExternalInput").ap()
    d_ones12 = nc.dram_tensor("ones12b", [12, 1], BF16, kind="ExternalInput").ap()
    d_ka = nc.dram_tensor("ka", [128, 8], F32, kind="ExternalInput").ap()
    d_ka2 = nc.dram_tensor("ka2", [128, 8], F32, kind="ExternalInput").ap()
    d_lx = nc.dram_tensor("lx", [1, 8], F32, kind="ExternalInput").ap()
    d_ck = nc.dram_tensor("Ck", [8, 128, 1024], BF16, kind="ExternalInput").ap()
    d_skn = nc.dram_tensor("SkN", [8, 128, 1024], BF16, kind="ExternalInput").ap()
    d_out = nc.dram_tensor("outT", [E, TC], F32, kind="ExternalOutput").ap()

    d_w1bf = nc.dram_tensor("w1bf", [NEXP, 8, 128, 4096], BF16, kind="Internal").ap()
    d_w2bf = nc.dram_tensor("w2bf", [NEXP, 8, 128, 4096], BF16, kind="Internal").ap()
    d_ws1bf = nc.dram_tensor("ws1bf", [8, 128, 4096], BF16, kind="Internal").ap()
    d_ws2bf = nc.dram_tensor("ws2bf", [8, 128, 4096], BF16, kind="Internal").ap()
    d_hh = nc.dram_tensor("hh", [2048], BF16, kind="Internal").ap()

    es = ExitStack()
    E_ = es.enter_context

    s_xT = [E_(nc.sbuf_tensor(f"s_xT{i}", [128, 8 * GRP], F32)) for i in range(2)]
    s_xnT = E_(nc.sbuf_tensor("s_xnT", [128, 8 * GRP], BF16))
    s_xnTr = E_(nc.sbuf_tensor("s_xnTr", [128, 8 * GRP], BF16))
    s_acc = E_(nc.sbuf_tensor("s_acc", [128, 8 * GRP], F32))
    s_gh = E_(nc.sbuf_tensor("s_gh", [128, 32 * GRP], BF16))
    s_gb = E_(nc.sbuf_tensor("s_gb", [128, 8 * GRP], BF16))
    s_w1b = [E_(nc.sbuf_tensor(f"s_w1b{i}", [128, 8 * 512], BF16)) for i in range(2)]
    s_w2b = [E_(nc.sbuf_tensor(f"s_w2b{i}", [128, 4 * 1024], BF16)) for i in range(2)]
    s_msb = [E_(nc.sbuf_tensor(f"s_msb{i}", [128, 1024], BF16)) for i in range(2)]
    s_x1 = E_(nc.sbuf_tensor("s_x1", [128, 8 * GRP], BF16))
    s_sT = E_(nc.sbuf_tensor("s_sT", [128, 8 * GRP], BF16))
    s_gtmp = [E_(nc.sbuf_tensor(f"s_gtmp{i}", [128, GRP], BF16)) for i in range(2)]
    s_sq = [E_(nc.sbuf_tensor(f"s_sq{i}", [128, GRP], F32)) for i in range(2)]
    s_xtmp = [E_(nc.sbuf_tensor(f"s_xtmp{i}", [128, GRP], F32)) for i in range(2)]
    s_gch = [E_(nc.sbuf_tensor(f"s_gch{i}", [128, GRP], BF16)) for i in range(2)]
    s_pch = [E_(nc.sbuf_tensor(f"s_pch{i}", [128, GRP], F32)) for i in range(2)]
    s_och = [E_(nc.sbuf_tensor(f"s_och{i}", [128, GRP], F32)) for i in range(2)]
    s_t1 = [E_(nc.sbuf_tensor(f"s_t1_{i}", [128, GRP], BF16)) for i in range(2)]

    s_wr = E_(nc.sbuf_tensor("s_wr", [128, 64], BF16))
    s_wel = E_(nc.sbuf_tensor("s_wel", [128, 8 * 24], BF16))
    s_wle = E_(nc.sbuf_tensor("s_wle", [24, E], BF16))
    s_b1 = E_(nc.sbuf_tensor("s_b1", [128, NEXP * 32], F32))
    s_bs1 = E_(nc.sbuf_tensor("s_bs1", [128, 32], F32))
    s_b2 = E_(nc.sbuf_tensor("s_b2", [8, E], BF16))
    s_bs2 = E_(nc.sbuf_tensor("s_bs2", [128, 8], F32))
    s_bs2s = E_(nc.sbuf_tensor("s_bs2s", [128, 8], F32))
    s_ble = E_(nc.sbuf_tensor("s_ble", [128, 8], F32))
    s_bel = E_(nc.sbuf_tensor("s_bel", [24, 1], F32))
    s_nbel = E_(nc.sbuf_tensor("s_nbel", [24, 1], F32))
    s_ht = E_(nc.sbuf_tensor("s_ht", [24, 12], BF16))
    s_id = E_(nc.sbuf_tensor("s_id", [128, 128], BF16))
    s_sel = E_(nc.sbuf_tensor("s_sel", [8, 8 * 128], BF16))
    s_onesc = E_(nc.sbuf_tensor("s_onesc", [128, 2], F32))
    s_onesr = E_(nc.sbuf_tensor("s_onesr", [1, 128], F32))
    s_onesb = E_(nc.sbuf_tensor("s_onesb", [1, 128], BF16))
    s_ones12 = E_(nc.sbuf_tensor("s_ones12", [12, 1], BF16))
    s_ka = E_(nc.sbuf_tensor("s_ka", [128, 8], F32))
    s_ka2 = E_(nc.sbuf_tensor("s_ka2", [128, 8], F32))
    s_lx = E_(nc.sbuf_tensor("s_lx", [1, 8], F32))

    s_nmax = E_(nc.sbuf_tensor("s_nmax", [128, 4], F32))
    s_esm = E_(nc.sbuf_tensor("s_esm", [128, 32], F32))
    s_ssum = E_(nc.sbuf_tensor("s_ssum", [128, 4], F32))
    s_srec = E_(nc.sbuf_tensor("s_srec", [128, 4], F32))
    s_probs = E_(nc.sbuf_tensor("s_probs", [128, 32], F32))
    s_v1 = E_(nc.sbuf_tensor("s_v1", [128, 4], F32))
    s_c1 = E_(nc.sbuf_tensor("s_c1", [128, 32], F32))
    s_p2 = E_(nc.sbuf_tensor("s_p2", [128, 32], F32))
    s_v2 = E_(nc.sbuf_tensor("s_v2", [128, 4], F32))
    s_gmask = E_(nc.sbuf_tensor("s_gmask", [128, 32], F32))
    s_gates = E_(nc.sbuf_tensor("s_gates", [128, 32], F32))
    s_gatesb = E_(nc.sbuf_tensor("s_gatesb", [128, 32], BF16))
    s_gT = E_(nc.sbuf_tensor("s_gT", [8, GRP], BF16))
    s_bitsf = E_(nc.sbuf_tensor("s_bitsf", [24, GRP], F32))
    s_bits = E_(nc.sbuf_tensor("s_bits", [24, GRP], BF16))
    s_synpf = E_(nc.sbuf_tensor("s_synpf", [24, GRP], F32))
    s_synp = E_(nc.sbuf_tensor("s_synp", [12, GRP], BF16))
    s_fxb = E_(nc.sbuf_tensor("s_fxb", [1, GRP], BF16))
    s_rT = E_(nc.sbuf_tensor("s_rT", [24, GRP], F32))
    s_uT = E_(nc.sbuf_tensor("s_uT", [24, GRP], F32))
    s_corr = E_(nc.sbuf_tensor("s_corr", [24, GRP], BF16))

    s_cnt = E_(nc.sbuf_tensor("s_cnt", [1, 8], F32))
    s_cnl = E_(nc.sbuf_tensor("s_cnl", [1, 8], F32))
    s_ptmp = E_(nc.sbuf_tensor("s_ptmp", [1, E], F32))
    s_dsrow = E_(nc.sbuf_tensor("s_dsrow", [1, 4], F32))
    s_bcD = E_(nc.sbuf_tensor("s_bcD", [128, 4], F32))
    s_ang = E_(nc.sbuf_tensor("s_ang", [128, 8], F32))
    s_ph = E_(nc.sbuf_tensor("s_ph", [128, 8], F32))
    s_mgt = E_(nc.sbuf_tensor("s_mgt", [128, 8], F32))
    s_zz = E_(nc.sbuf_tensor("s_zz", [128, 8], F32))
    s_sgn = E_(nc.sbuf_tensor("s_sgn", [128, 8], F32))
    s_sa = E_(nc.sbuf_tensor("s_sa", [128, 8], F32))
    s_sing = E_(nc.sbuf_tensor("s_sing", [128, 8], F32))
    s_cosp = E_(nc.sbuf_tensor("s_cosp", [128, 8], F32))
    s_sinp = E_(nc.sbuf_tensor("s_sinp", [128, 8], F32))
    s_rf = E_(nc.sbuf_tensor("s_rf", [128, 8], F32))
    s_if = E_(nc.sbuf_tensor("s_if", [128, 8], F32))
    s_rfb = E_(nc.sbuf_tensor("s_rfb", [128, 8], BF16))
    s_ifb = E_(nc.sbuf_tensor("s_ifb", [128, 8], BF16))
    s_hreb = E_(nc.sbuf_tensor("s_hreb", [1, E], BF16))

    banks = [E_(nc.psum_tensor(f"bank{i}", [128, 512], F32)) for i in range(8)]

    s_xn00 = s_acc[0:1, 0:1024]
    s_dsc = s_acc[0:1, 1536:1568]
    s_mrow = s_acc[0:1, 0:512]
    s_vrow = s_acc[0:1, 512:1024]
    s_rrow = s_acc[0:1, 1024:1536]

    SP = Sem("SP"); SA = Sem("SA"); SV = Sem("SV")
    PC = Sem("PC"); PK = Sem("PK"); PH = Sem("PH")
    PT = [Sem("PTa"), Sem("PTb")]
    PR = [Sem("PRa"), Sem("PRb")]
    PM = [Sem("PMa"), Sem("PMb")]
    SO = [Sem("SOa"), Sem("SOb")]
    PWC = Sem("PWC")
    PW1 = [Sem("PW1a"), Sem("PW1b")]
    PW2 = [Sem("PW2a"), Sem("PW2b")]
    PS2 = [Sem("PS2a"), Sem("PS2b")]
    PB1 = [Sem("PB1a"), Sem("PB1b")]
    PB2 = [Sem("PB2a"), Sem("PB2b")]
    for s in (SP, SA, SV, PC, PK, PH, *PT, *PR, *PM, *SO, PWC, *PW1, *PW2,
              *PS2, *PB1, *PB2):
        s.h = es.enter_context(nc.semaphore(s.name))

    blk = es.enter_context(nc.Block())

    def ch(t, dc, w=GRP):
        return t[:, dc * w:(dc + 1) * w]

    def chx(grp, dc):
        return s_xT[grp % 2][:, dc * GRP:(dc + 1) * GRP]

    NOUT_DMAS = NG * 8

    def stream_sync(em):
        g = em.eng
        em.do(lambda: g.dma_start(s_onesc[:], d_ones)); em.inc(PC, None, 16)
        em.do(lambda: g.dma_start(s_onesr[:], d_onesr)); em.inc(PC, None, 16)
        em.do(lambda: g.dma_start(s_onesb[:], d_onesb)); em.inc(PC, None, 16)
        em.do(lambda: g.dma_start(s_ones12[:], d_ones12)); em.inc(PC, None, 16)
        em.do(lambda: g.dma_start(s_ka[:], d_ka)); em.inc(PC, None, 16)
        em.do(lambda: g.dma_start(s_ka2[:], d_ka2)); em.inc(PC, None, 16)
        em.do(lambda: g.dma_start(s_lx[:], d_lx)); em.inc(PC, None, 16)
        em.do(lambda: g.dma_start(s_ht[:], d_ht)); em.inc(PC, None, 16)
        em.do(lambda: g.dma_start(s_id[:], d_ident)); em.inc(PC, None, 16)
        em.do(lambda: g.dma_start(s_sel[:], d_sel)); em.inc(PC, None, 16)
        em.do(lambda: g.dma_start(
            s_b1[:].rearrange("p (e j) -> p e j", e=NEXP),
            d_b1.rearrange("e p j -> p e j")))
        em.inc(PC, None, 16)
        em.do(lambda: g.dma_start(s_bs1[:], d_bs1)); em.inc(PC, None, 16)
        em.do(lambda: g.dma_start(s_bs2[:], d_bs2)); em.inc(PC, None, 16)
        em.do(lambda: g.dma_start(s_ble[:], d_ble)); em.inc(PC, None, 16)
        em.do(lambda: g.dma_start(s_bel[:], d_bel)); em.inc(PC, None, 16)
        em.do(lambda: g.dma_start(s_ptmp[:], d_xfirst)); em.inc(PC, ("consts",), 16)
        for c in range(8):
            if c > 0:
                em.wait(SP, ("hre", c - 1))
            em.do(lambda c=c: g.dma_start(s_msb[0][:], d_ck[c]))
            em.inc(PK, None, 16)
            em.do(lambda c=c: g.dma_start(s_msb[1][:], d_skn[c]))
            em.inc(PK, ("ckskn", c), 16)
        em.wait(SA, ("hrev", 1))
        em.do(lambda: g.dma_start(bass.AP(d_hh.tensor, 0, [[1, 1], [1, 1024]]), s_hreb[0:1, :]))
        em.inc(PH, None, 16)
        em.do(lambda: g.dma_start(bass.AP(d_hh.tensor, 1024, [[1, 1], [1, 1024]]), s_hreb[0:1, :]))
        em.inc(PH, ("hh",), 16)

        for grp in range(2):
            for dc in range(8):
                em.do(lambda grp=grp, dc=dc: g.dma_start(
                    s_xT[grp % 2][:, dc * GRP:(dc + 1) * GRP],
                    d_xT[dc * 128:(dc + 1) * 128, grp * GRP:(grp + 1) * GRP]))
                em.inc(PT[grp % 2], ("xTg", grp) if dc == 7 else None, 16)

        for grp in range(NG):
            for dc in range(8):
                if dc < 2:
                    em.wait(SP, ("sqsum", grp, 7))
                else:
                    em.wait(SV, ("xnr", grp, dc - 2))
                em.do(lambda grp=grp, dc=dc: g.dma_start(
                    s_sq[dc % 2][:],
                    d_xTrev[dc * 128:(dc + 1) * 128, grp * GRP:(grp + 1) * GRP]))
                em.inc(PR[dc % 2], ("xTr", grp, dc), 16)
            if grp == 0:
                for e in range(NE):
                    for fb in range(8):
                        k = e * 8 + fb
                        em.wait(PW1[k % 2], ("w1", 0, e, fb))
                        dst = d_ws1bf[fb] if e == 0 else d_w1bf[e - 1][fb]
                        em.do(lambda dst=dst, b=(k + 1) % 2: g.dma_start(
                            dst, s_w1b[b][:]))
                        em.inc(PB1[k % 2], ("wb1", k), 16)
                    for pk in range(8):
                        k = e * 8 + pk
                        em.wait(PW2[k % 2], ("w2", 0, e, pk))
                        dst = d_ws2bf[pk] if e == 0 else d_w2bf[e - 1][pk]
                        em.do(lambda dst=dst, b=(k + 1) % 2: g.dma_start(
                            dst, s_w2b[b][:]))
                        em.inc(PB2[k % 2], ("wb2", k), 16)
            for m in range(8):
                if grp == 0 and m < 2:
                    em.wait(PH, ("hh",))
                    em.wait(SP, ("hre", 7))
                else:
                    pm = (grp * 8 + m) - 2
                    em.wait(SP, ("convm", pm // 8, pm % 8))
                em.do(lambda m=m: g.dma_start(
                    s_msb[m % 2][:],
                    bass.AP(d_hh.tensor, m * 128 + 1, [[1, 128], [1, 1024]])))
                em.inc(PM[m % 2], ("ms", grp, m), 16)
            for dp in range(8):
                em.wait(SV, ("out", grp, dp))
                em.do(lambda grp=grp, dp=dp: g.dma_start(
                    d_out[dp * 128:(dp + 1) * 128, grp * GRP:(grp + 1) * GRP],
                    s_och[dp % 2][:]))
                em.inc(SO[dp % 2], ("st", grp, dp), 16)
            if grp + 2 < NG:
                for dc in range(8):
                    em.do(lambda grp=grp, dc=dc: g.dma_start(
                        s_xT[grp % 2][:, dc * GRP:(dc + 1) * GRP],
                        d_xT[dc * 128:(dc + 1) * 128,
                             (grp + 2) * GRP:(grp + 3) * GRP]))
                    em.inc(PT[grp % 2], ("xTg", grp + 2) if dc == 7 else None, 16)
        em.wait_val(SO[0], 16 * (NOUT_DMAS // 2))
        em.wait_val(SO[1], 16 * (NOUT_DMAS // 2))

    def stream_gpsimd(em):
        g = em.eng
        em.do(lambda: g.dma_start(
            s_wr[:].rearrange("p (c j) -> p c j", c=8),
            d_wr.rearrange("c p j -> p c j")))
        em.inc(PWC, None, 16)
        em.do(lambda: g.dma_start(
            s_wel[:].rearrange("p (c j) -> p c j", c=8),
            d_wel.rearrange("c p j -> p c j")))
        em.inc(PWC, None, 16)
        em.do(lambda: g.dma_start(s_wle[:], d_wle)); em.inc(PWC, None, 16)
        em.do(lambda: g.dma_start(s_b2[:], d_b2)); em.inc(PWC, ("smallcast",), 16)

        wslab = 0
        wpack = 0
        for grp in range(NG):
            for e in range(NE):
                for fb in range(8):
                    wslab += 1
                    if wslab > 2:
                        i2 = wslab - 2
                        g2, r2 = divmod(i2 - 1, NE * 8)
                        e2, fb2 = divmod(r2, 8)
                        em.wait(SP, ("mm1", g2, e2, fb2 * 4 + 3))
                        if i2 <= NE * 8:
                            em.wait(PB1[(i2 - 1) % 2], ("wb1", i2 - 1))
                    if grp == 0:
                        w1src = d_ws1 if e == 0 else d_w1[e - 1]
                    else:
                        w1src = d_ws1bf if e == 0 else d_w1bf[e - 1]
                    em.do(lambda w1src=w1src, fb=fb, b=wslab % 2: g.dma_start(
                        s_w1b[b][:], w1src[fb]))
                    em.inc(PW1[(wslab - 1) % 2], ("w1", grp, e, fb), 16)
                if grp == 0:
                    for pk in range(8):
                        wpack += 1
                        if wpack > 2:
                            j2 = wpack - 2
                            e2, pk2 = divmod(j2 - 1, 8)
                            em.wait(SP, ("mm2p", 0, e2, pk2))
                            em.wait(PB2[(j2 - 1) % 2], ("wb2", j2 - 1))
                        w2src = d_ws2 if e == 0 else d_w2[e - 1]
                        em.do(lambda w2src=w2src, pk=pk, b=wpack % 2: g.dma_start(
                            s_w2b[b][:], w2src[pk]))
                        em.inc(PW2[(wpack - 1) % 2], ("w2", 0, e, pk), 16)

    def stream_tensor(em):
        t = em.eng
        bfree = {i: None for i in range(8)}

        def use_bank(i):
            if bfree[i] is not None:
                em.wait(*bfree[i])
                bfree[i] = None

        em.wait(SV, ("dsrow",))
        em.do(lambda: t.matmul(banks[2][0:128, 0:4], s_onesr[:], s_dsrow[:],
                               start=True, stop=True))
        em.inc(SP, ("bcD",), 1)
        em.wait(SV, ("ifb",))
        for c in range(8):
            em.wait(PK, ("ckskn", c))
            for half in range(2):
                em.do(lambda c=c, half=half: t.matmul(
                    banks[half][0:1, 0:512], s_rfb[:, c:c + 1],
                    s_msb[0][:, half * 512:(half + 1) * 512],
                    start=(c == 0), stop=False))
                em.do(lambda c=c, half=half: t.matmul(
                    banks[half][0:1, 0:512], s_ifb[:, c:c + 1],
                    s_msb[1][:, half * 512:(half + 1) * 512],
                    start=False, stop=(c == 7)))
            em.inc(SP, ("hre", c), 1)
        bfree[0] = (SA, ("hrev", 0))
        bfree[1] = (SA, ("hrev", 1))
        bfree[2] = (SA, ("bcDev",))
        em.wait(PWC, ("smallcast",))

        for grp in range(NG):
            use_bank(6)
            em.wait(PT[grp % 2], ("xTg", grp))
            for dc in range(8):
                em.do(lambda grp=grp, dc=dc: t.matmul(
                    banks[6][0:1, 0:GRP], s_onesc[:, 0:1], chx(grp, dc),
                    start=(dc == 0), stop=(dc == 7)))
            em.inc(SP, ("msum", grp), 1)
            use_bank(7)
            for dc in range(8):
                em.wait(SA, ("sq", grp, dc))
                em.do(lambda dc=dc: t.matmul(
                    banks[7][0:1, 0:GRP], s_onesc[:, 0:1], s_sq[dc % 2][:],
                    start=(dc == 0), stop=(dc == 7)))
                em.inc(SP, ("sqsum", grp, dc), 1)
            em.wait(SV, ("mmean", grp))
            em.do(lambda: t.matmul(banks[6][:, 0:GRP], s_onesr[:], s_mrow[:],
                                   start=True, stop=True))
            em.inc(SP, ("mb", grp), 1)
            em.wait(SA, ("rstd", grp))
            em.do(lambda: t.matmul(banks[7][:, 0:GRP], s_onesr[:], s_rrow[:],
                                   start=True, stop=True))
            em.inc(SP, ("rb", grp), 1)
            bfree[6] = (SV, ("xnr", grp, 7))
            bfree[7] = (SV, ("xnr", grp, 7))
            use_bank(5)
            for tt in range(4):
                for dc in range(8):
                    if tt == 0:
                        em.wait(SV, ("xn", grp, dc))
                    em.do(lambda tt=tt, dc=dc: t.matmul(
                        banks[5][0:128, tt * 128:tt * 128 + 8],
                        s_xnT[:, dc * GRP + tt * 128: dc * GRP + (tt + 1) * 128],
                        s_wr[:, dc * 8:(dc + 1) * 8],
                        start=(dc == 0), stop=(dc == 7)))
                em.inc(SP, ("logits", grp, tt), 1)
            bfree[5] = (SA, ("exp", grp, 3))

            for e in range(NE):
                w1buf = None
                for f in range(32):
                    fb, k = divmod(f, 4)
                    if k == 0:
                        em.wait(PW1[(grp * NE * 8 + e * 8 + fb) % 2],
                                ("w1", grp, e, fb))
                        w1buf = s_w1b[(grp * NE * 8 + e * 8 + fb + 1) % 2]
                    use_bank(f % 2)
                    for dc in range(8):
                        em.do(lambda w1buf=w1buf, f=f, dc=dc, k=k: t.matmul(
                            banks[f % 2][:],
                            w1buf[:, dc * 512 + k * 128: dc * 512 + (k + 1) * 128],
                            ch(s_xnT, dc),
                            start=(dc == 0), stop=(dc == 7)))
                    em.inc(SP, ("mm1", grp, e, f), 1)
                    bfree[f % 2] = (SA, ("gelu", grp, e, f))
                if e == 0:
                    use_bank(4)
                    for tt in range(4):
                        em.wait(SV, ("gatesb", grp, tt))
                        em.do(lambda tt=tt: t.transpose(
                            banks[4][0:128, 0:256].bitcast(BF16)[0:8, tt * 128:(tt + 1) * 128],
                            s_gatesb[:, tt * 8:(tt + 1) * 8], s_id[:]))
                        em.inc(SP, ("gtr", grp, tt), 1)
                    bfree[4] = (SA, ("gTev", grp))
                    use_bank(3)
                    for ge in range(8):
                        if ge == 0:
                            em.wait(SA, ("gTev", grp))
                        else:
                            em.wait(SA, ("gbev", grp, ge - 1))
                        em.do(lambda ge=ge: t.matmul(
                            banks[3][:, 0:GRP], s_sel[:, ge * 128:(ge + 1) * 128],
                            s_gT[:], start=True, stop=True))
                        em.inc(SP, ("gbc", grp, ge), 1)
                    bfree[3] = (SA, ("gbev", grp, 7))
                em.wait(SA, ("gelu", grp, e, 30))
                em.wait(SA, ("gelu", grp, e, 31))
                bfree[0] = None
                bfree[1] = None
                for i in range(2, 8):
                    use_bank(i)
                w2buf = None
                for f in range(32):
                    pk, j = divmod(f, 4)
                    if j == 0:
                        idx = grp * NE * 8 + e * 8 + pk
                        em.wait(PW2[idx % 2] if grp == 0 else PS2[idx % 2],
                                ("w2", grp, e, pk))
                        w2buf = s_w2b[(idx + 1) % 2]
                    if e == 0:
                        em.wait(SA, ("gelu", grp, 0, f))
                    else:
                        em.wait(SV, ("gh", grp, e, f))
                    for dp in range(8):
                        em.do(lambda w2buf=w2buf, j=j, dp=dp, f=f, e=e: t.matmul(
                            banks[dp][:],
                            w2buf[:, j * 1024 + dp * 128: j * 1024 + (dp + 1) * 128],
                            s_gh[:, f * GRP:(f + 1) * GRP],
                            start=(f == 0),
                            stop=(f == 31 and e != NE - 1)))
                    if j == 3:
                        em.inc(SP, ("mm2p", grp, e, pk), 1)
                if e == NE - 1:
                    for dp in range(8):
                        em.do(lambda dp=dp: t.matmul(
                            banks[dp][:], s_b2[:, dp * 128:(dp + 1) * 128],
                            s_gT[:], start=False, stop=True))
                    em.inc(SP, ("mm2end", grp, e), 1)
                else:
                    em.alias(SP, ("mm2end", grp, e))
                for i in range(8):
                    bfree[i] = (SV, ("yev", grp, e, i))

            for i in range(8):
                use_bank(i)
            for m in range(8):
                em.wait(PM[m % 2], ("ms", grp, m))
                em.wait(SV, ("xnr", grp, m))
                for n in range(8):
                    em.do(lambda m=m, n=n: t.matmul(
                        banks[n][:], s_msb[m % 2][:, n * 128:(n + 1) * 128],
                        ch(s_xnTr, m),
                        start=(m == 0), stop=(m == 7)))
                em.inc(SP, ("convm", grp, m), 1)
            for n in range(8):
                bfree[n] = (SA, ("x1", grp, n))
            use_bank(0)
            for dc in range(8):
                em.wait(SA, ("x1", grp, dc))
                em.do(lambda dc=dc: t.matmul(
                    banks[0][0:24, 0:GRP], s_wel[:, dc * 24:(dc + 1) * 24],
                    ch(s_x1, dc), start=(dc == 0), stop=(dc == 7)))
            em.inc(SP, ("lat", grp), 1)
            bfree[0] = (SV, ("bits", grp))
            use_bank(1)
            em.wait(SV, ("bits", grp))
            em.do(lambda: t.matmul(banks[1][0:12, 0:GRP], s_ht[:], s_bits[:],
                                   start=True, stop=True))
            em.inc(SP, ("syn", grp), 1)
            bfree[1] = (SV, ("synp", grp))
            use_bank(2)
            em.wait(SV, ("synp", grp))
            em.do(lambda: t.matmul(banks[2][0:1, 0:GRP], s_ones12[:],
                                   s_synp[:], start=True, stop=True))
            em.inc(SP, ("wsum", grp), 1)
            bfree[2] = (SV, ("fix", grp))
            use_bank(3)
            em.wait(SV, ("fix", grp))
            em.do(lambda: t.matmul(banks[3][0:24, 0:GRP], s_onesb[0:1, 0:24],
                                   s_fxb[:], start=True, stop=True))
            em.inc(SP, ("fixbc", grp), 1)
            bfree[3] = (SV, ("corr", grp))
            em.wait(SV, ("corr", grp))
            for dp in range(8):
                use_bank(4 + dp % 2)
                em.do(lambda dp=dp: t.matmul(
                    banks[4 + dp % 2][:], s_wle[:, dp * 128:(dp + 1) * 128],
                    s_corr[:], start=True, stop=True))
                em.inc(SP, ("smm", grp, dp), 1)
                bfree[4 + dp % 2] = (SV, ("t1", grp, dp))

    def stream_scalar(em):
        sc = em.eng
        em.wait(PC, ("consts",))
        em.do(lambda: sc.activation(s_xn00[:], s_ptmp[:], AF.Square,
                                    accum_out=s_dsc[0:1, 1:2]))
        em.inc(SA, ("xfsq",), 1)
        em.wait(SV, ("cnt",))
        em.do(lambda: sc.activation(s_cnl[0:1, 0:5], s_cnt[0:1, 0:5], AF.Ln))
        em.inc(SA, ("cntln",), 1)
        em.wait(SP, ("bcD",))
        em.do(lambda: sc.activation(s_bcD[:], banks[2][0:128, 0:4], AF.Copy))
        em.inc(SA, ("bcDev",), 1)
        em.wait(SV, ("zz",))
        em.do(lambda: sc.activation(s_sa[:], s_zz[:], AF.Sin))
        em.inc(SA, ("sinzz",), 1)
        em.wait(SV, ("ph",))
        em.do(lambda: sc.activation(s_cosp[:], s_ph[:], AF.Sin, scale=-1.0, bias=s_onesc[:, 1:2]))
        em.inc(SA, ("cosp",), 1)
        em.do(lambda: sc.activation(s_sinp[:], s_ph[:], AF.Sin))
        em.inc(SA, ("sinp",), 1)
        em.wait(SP, ("hre", 7))
        for half in range(2):
            em.do(lambda half=half: sc.activation(
                s_hreb[0:1, half * 512:(half + 1) * 512], banks[half][0:1, 0:512],
                AF.Copy, scale=1.0 / E))
            em.inc(SA, ("hrev", half), 1)

        def w2load(grp, e, pk):
            idx = grp * NE * 8 + e * 8 + pk
            g2, r2 = divmod(idx - 2, NE * 8)
            e2, pk2 = divmod(r2, 8)
            em.wait(SP, ("mm2p", g2, e2, pk2))
            w2src = d_ws2bf if e == 0 else d_w2bf[e - 1]
            em.do(lambda w2src=w2src, pk=pk, b=(idx + 1) % 2: sc.dma_start(
                s_w2b[b][:], w2src[pk]))
            em.inc(PS2[idx % 2], ("w2", grp, e, pk), 16)

        gtmp_free = {0: None, 1: None}
        for grp in range(NG):
            for dc in range(8):
                if dc == 0:
                    em.wait(PT[grp % 2], ("xTg", grp))
                if dc >= 2:
                    em.wait(SP, ("sqsum", grp, dc - 2))
                elif grp > 0:
                    em.wait(SV, ("xnr", grp - 1, 6 + dc))
                em.do(lambda grp=grp, dc=dc: sc.activation(
                    s_sq[dc % 2][:], chx(grp, dc), AF.Square))
                em.inc(SA, ("sq", grp, dc), 1)
            if grp > 0:
                w2load(grp, 0, 0)
                w2load(grp, 0, 1)
            em.wait(SV, ("var", grp))
            em.do(lambda: sc.activation(s_vrow[:], s_vrow[:], AF.Ln, bias=s_lx[0:1, 5:6]))
            em.inc(SA, ("lnv", grp), 1)
            em.wait(SA, ("lnv", grp))
            em.do(lambda: sc.activation(s_rrow[:], s_vrow[:], AF.Exp, scale=-0.5))
            em.inc(SA, ("rstd", grp), 1)
            for tt in range(4):
                em.wait(SV, ("negmax", grp, tt))
                em.do(lambda tt=tt: sc.activation(
                    s_esm[:, tt * 8:(tt + 1) * 8],
                    banks[5][0:128, tt * 128:tt * 128 + 8],
                    AF.Exp, bias=s_nmax[:, tt:tt + 1]))
                em.inc(SA, ("exp", grp, tt), 1)
            for f in range(32):
                em.wait(SP, ("mm1", grp, 0, f))
                em.do(lambda f=f: sc.activation(
                    s_gh[:, f * GRP:(f + 1) * GRP], banks[f % 2][:],
                    AF.Gelu_apprx_tanh, bias=s_bs1[:, f:f + 1]))
                em.inc(SA, ("gelu", grp, 0, f), 1)
            for tt in range(4):
                em.wait(SP, ("gtr", grp, tt))
            em.do(lambda: sc.activation(
                s_gT[:], banks[4][0:128, 0:256].bitcast(BF16)[0:8, 0:512], AF.Copy))
            em.inc(SA, ("gTev", grp), 1)
            for ge in range(8):
                em.wait(SP, ("gbc", grp, ge))
                if grp > 0:
                    em.wait(SV, ("gh", grp - 1, NE - 1, 31))
                em.do(lambda ge=ge: sc.activation(
                    ch(s_gb, ge), banks[3][:, 0:GRP], AF.Copy))
                em.inc(SA, ("gbev", grp, ge), 1)
            if grp > 0:
                for pk in range(2, 8):
                    w2load(grp, 0, pk)
            for e in range(1, NE):
                for f in range(32):
                    em.wait(SP, ("mm1", grp, e, f))
                    if gtmp_free[f % 2] is not None:
                        em.wait(SV, gtmp_free[f % 2])
                    em.do(lambda e=e, f=f: sc.activation(
                        s_gtmp[f % 2][:], banks[f % 2][:],
                        AF.Gelu_apprx_tanh,
                        bias=s_b1[:, (e - 1) * 32 + f:(e - 1) * 32 + f + 1]))
                    em.inc(SA, ("gelu", grp, e, f), 1)
                    gtmp_free[f % 2] = ("gh", grp, e, f)
                    if grp > 0 and f == 0:
                        w2load(grp, e, 0)
                        w2load(grp, e, 1)
                if grp > 0:
                    for pk in range(2, 8):
                        w2load(grp, e, pk)
            em.wait(SP, ("convm", grp, 7))
            for n in range(8):
                em.do(lambda n=n: sc.activation(ch(s_x1, n), banks[n][:], AF.Copy))
                em.inc(SA, ("x1", grp, n), 1)
            for dp in range(8):
                em.wait(SV, ("yev", grp, NE - 1, dp))
                if grp > 0 or dp >= 2:
                    pv = grp * 8 + dp - 2
                    em.wait(SV, ("prod", pv // 8, pv % 8))
                em.do(lambda dp=dp: sc.activation(
                    s_gch[dp % 2][:], ch(s_acc, dp), AF.Silu, scale=SCALE,
                    bias=s_bs2s[:, dp:dp + 1]))
                em.inc(SA, ("silu", grp, dp), 1)

    def stream_vector(em):
        v = em.eng
        kc = [0]

        def step(fn, name=None, *, after=()):
            for w in after:
                em.wait(SV, w)
            em.do(fn)
            if name is None:
                kc[0] += 1
                name = ("v", kc[0])
            em.inc(SV, name, 1)
            return name

        em.wait(PC, ("consts",))
        step(lambda: v.tensor_scalar(s_bs2s[:], s_bs2[:], SCALE, None, ALU.mult))
        l_nbel = step(lambda: v.tensor_scalar(s_nbel[:], s_bel[:], -1.0, None, ALU.mult))
        l_syn0 = step(lambda: v.memset(s_synpf[:], 0.0), ("init",))
        l_sum = step(lambda: v.reduce_sum(s_dsc[0:1, 0:1], s_ptmp[:], axis=AXL.X))
        em.wait(SA, ("xfsq",))
        l_mean = step(lambda: v.tensor_scalar(s_dsc[0:1, 2:3], s_dsc[0:1, 0:1], 1.0 / E, None, ALU.mult), after=(l_sum,))
        l_xn00 = step(lambda: v.tensor_scalar(s_xn00[:], s_ptmp[:], s_dsc[0:1, 2:3], None, ALU.subtract), ("xn00",), after=(l_mean,))
        prev = l_xn00
        for i, n in enumerate([4, 16, 64, 256, 1024]):
            l_bm = step(lambda i=i, n=n: v.reduce_sum(s_dsc[0:1, 8 + i:9 + i], s_xn00[0:1, 0:n], axis=AXL.X), after=(prev,))
            l_bm2 = step(lambda i=i: v.tensor_scalar(s_dsc[0:1, 8 + i:9 + i], s_dsc[0:1, 8 + i:9 + i], 1.0 / n, None, ALU.mult), after=(l_bm,))
            l_gt = step(lambda i=i, n=n: v.tensor_scalar(s_ptmp[0:1, 0:n], s_xn00[0:1, 0:n], s_dsc[0:1, 8 + i:9 + i], None, ALU.is_gt), after=(l_bm2,))
            l_ct = step(lambda i=i, n=n: v.reduce_sum(s_cnt[0:1, i:i + 1], s_ptmp[0:1, 0:n], axis=AXL.X), after=(l_gt,))
            prev = step(lambda i=i: v.tensor_scalar(s_cnt[0:1, i:i + 1], s_cnt[0:1, i:i + 1], 1.0, None, ALU.max), after=(l_ct,))
        em.alias(SV, ("cnt",))
        em.wait(SA, ("cntln",))
        l1 = step(lambda: v.tensor_tensor(s_ptmp[0:1, 0:5], s_cnl[0:1, 0:5], s_lx[0:1, 0:5], ALU.mult))
        l2 = step(lambda: v.reduce_sum(s_dsc[0:1, 16:17], s_ptmp[0:1, 0:5], axis=AXL.X), after=(l1,))
        l3 = step(lambda: v.reduce_sum(s_dsc[0:1, 17:18], s_cnl[0:1, 0:5], axis=AXL.X))
        l4 = step(lambda: v.tensor_scalar(s_dsc[0:1, 18:19], s_dsc[0:1, 16:17], 5.0, None, ALU.mult), after=(l2,))
        l5 = step(lambda: v.tensor_scalar(s_dsc[0:1, 19:20], s_dsc[0:1, 17:18], SLX, None, ALU.mult), after=(l3,))
        l6 = step(lambda: v.tensor_tensor(s_dsc[0:1, 20:21], s_dsc[0:1, 18:19], s_dsc[0:1, 19:20], ALU.subtract), after=(l5,))
        l7 = step(lambda: v.tensor_scalar(s_dsc[0:1, 21:22], s_dsc[0:1, 20:21], -1.0 / DEN, None, ALU.mult), after=(l6,))
        l8 = step(lambda: v.tensor_scalar(s_dsc[0:1, 21:22], s_dsc[0:1, 21:22], 1.0, None, ALU.max), after=(l7,))
        l9 = step(lambda: v.tensor_scalar(s_dsc[0:1, 21:22], s_dsc[0:1, 21:22], 2.0, None, ALU.min), after=(l8,))
        l10 = step(lambda: v.tensor_tensor(s_dsc[0:1, 22:23], s_dsc[0:1, 21:22], s_dsc[0:1, 21:22], ALU.mult), after=(l9,))
        l11 = step(lambda: v.tensor_scalar(s_dsrow[0:1, 0:1], s_dsc[0:1, 22:23], 0.2, None, ALU.mult), after=(l10,))
        l12 = step(lambda: v.tensor_tensor(s_dsrow[0:1, 0:1], s_dsrow[0:1, 0:1], s_dsc[0:1, 21:22], ALU.add), after=(l11,))
        step(lambda: v.tensor_scalar(s_dsrow[0:1, 1:2], s_dsc[0:1, 21:22], 0.5, 1.0, ALU.mult, ALU.add))
        step(lambda: v.tensor_scalar(s_dsrow[0:1, 2:3], s_dsc[0:1, 21:22], 0.1, None, ALU.mult))
        step(lambda: v.tensor_scalar(s_dsrow[0:1, 3:4], s_dsc[0:1, 21:22], 0.0, None, ALU.mult), ("dsrow",))
        em.wait(SA, ("bcDev",))
        l_ang = step(lambda: v.tensor_scalar(s_ang[:], s_ka[:], s_bcD[:, 1:2], s_bcD[:, 0:1], ALU.mult, ALU.add))
        l_ph1 = step(lambda: v.tensor_scalar(s_ph[:], s_ka2[:], s_bcD[:, 2:3], None, ALU.mult))
        l_ph2 = step(lambda: v.tensor_tensor(s_ph[:], s_ph[:], s_ka[:], ALU.subtract), after=(l_ph1,))
        step(lambda: v.tensor_scalar(s_ph[:], s_ph[:], s_bcD[:, 0:1], None, ALU.add), ("ph",), after=(l_ph2,))
        l_mgt = step(lambda: v.tensor_scalar(s_mgt[:], s_ang[:], PI, None, ALU.is_gt), after=(l_ang,))
        step(lambda: v.scalar_tensor_tensor(s_zz[:], s_mgt[:], -PI, s_ang[:], ALU.mult, ALU.add), ("zz",), after=(l_mgt,))
        l_sgn = step(lambda: v.tensor_scalar(s_sgn[:], s_mgt[:], -2.0, 1.0, ALU.mult, ALU.add))
        em.wait(SA, ("sinzz",))
        l_sing = step(lambda: v.tensor_tensor(s_sing[:], s_sa[:], s_sgn[:], ALU.mult), after=(l_sgn,))
        em.wait(SA, ("cosp",))
        l_rf = step(lambda: v.tensor_tensor(s_rf[:], s_sing[:], s_cosp[:], ALU.mult), after=(l_sing,))
        step(lambda: v.tensor_copy(s_rfb[:], s_rf[:]), ("rfb",), after=(l_rf,))
        em.wait(SA, ("sinp",))
        l_if = step(lambda: v.tensor_tensor(s_if[:], s_sing[:], s_sinp[:], ALU.mult))
        step(lambda: v.tensor_copy(s_ifb[:], s_if[:]), ("ifb",), after=(l_if,))

        for grp in range(NG):
            em.wait(SP, ("msum", grp))
            if grp > 0:
                em.wait(SA, ("silu", grp - 1, 2))
            l_mm = step(lambda: v.tensor_scalar(s_mrow[:], banks[6][0:1, 0:GRP], 1.0 / E, None, ALU.mult), ("mmean", grp))
            em.wait(SP, ("sqsum", grp, 7))
            l_vr = step(lambda: v.tensor_scalar(s_vrow[:], banks[7][0:1, 0:GRP], 1.0 / E, None, ALU.mult))
            l_pt = step(lambda: v.tensor_tensor(s_ptmp[0:1, 0:GRP], s_mrow[:], s_mrow[:], ALU.mult),
                        after=(l_mm,) if grp else (l_mm, l2))
            step(lambda: v.tensor_tensor(s_vrow[:], s_vrow[:], s_ptmp[0:1, 0:GRP], ALU.subtract), ("var", grp), after=(l_pt,))
            em.wait(SP, ("mb", grp))
            em.wait(SP, ("rb", grp))
            lt = {}
            for dc in range(8):
                lt[dc] = step(lambda grp=grp, dc=dc: v.tensor_tensor(
                    s_xtmp[dc % 2][:], chx(grp, dc), banks[6][:, 0:GRP], ALU.subtract),
                    after=(("xn", grp, dc - 2),) if dc >= 2 else ())
                if dc >= 1:
                    step(lambda dc=dc: v.tensor_tensor(
                        ch(s_xnT, dc - 1), s_xtmp[(dc - 1) % 2][:], banks[7][:, 0:GRP], ALU.mult),
                        ("xn", grp, dc - 1), after=(lt[dc - 1],))
            step(lambda: v.tensor_tensor(
                ch(s_xnT, 7), s_xtmp[1][:], banks[7][:, 0:GRP], ALU.mult),
                ("xn", grp, 7), after=(lt[7],))
            lr = {}
            for dc in range(8):
                em.wait(PR[dc % 2], ("xTr", grp, dc))
                lr[dc] = step(lambda dc=dc: v.tensor_tensor(
                    s_xtmp[dc % 2][:], s_sq[dc % 2][:], banks[6][:, 0:GRP], ALU.subtract),
                    after=(("xnr", grp, dc - 2),) if dc >= 2 else
                          (("xn", grp, 6 + dc),))
                if dc >= 1:
                    step(lambda dc=dc: v.tensor_tensor(
                        ch(s_xnTr, dc - 1), s_xtmp[(dc - 1) % 2][:], banks[7][:, 0:GRP], ALU.mult),
                        ("xnr", grp, dc - 1), after=(lr[dc - 1],))
            step(lambda: v.tensor_tensor(
                ch(s_xnTr, 7), s_xtmp[1][:], banks[7][:, 0:GRP], ALU.mult),
                ("xnr", grp, 7), after=(lr[7],))
            for tt in range(4):
                em.wait(SP, ("logits", grp, tt))
                step(lambda tt=tt: v.reduce_max(
                    s_nmax[:, tt:tt + 1], banks[5][0:128, tt * 128:tt * 128 + 8],
                    axis=AXL.X, negate=True), ("negmax", grp, tt))
            for tt in range(4):
                sl = slice(tt * 8, (tt + 1) * 8)
                em.wait(SA, ("exp", grp, tt))
                a1 = step(lambda sl=sl, tt=tt: v.reduce_sum(s_ssum[:, tt:tt + 1], s_esm[:, sl], axis=AXL.X))
                a2 = step(lambda tt=tt: v.reciprocal(s_srec[:, tt:tt + 1], s_ssum[:, tt:tt + 1]), after=(a1,))
                a3 = step(lambda sl=sl, tt=tt: v.tensor_scalar(s_probs[:, sl], s_esm[:, sl], s_srec[:, tt:tt + 1], None, ALU.mult), after=(a2,))
                a4 = step(lambda sl=sl, tt=tt: v.reduce_max(s_v1[:, tt:tt + 1], s_probs[:, sl], axis=AXL.X), after=(a3,))
                a5 = step(lambda sl=sl, tt=tt: v.tensor_scalar(s_c1[:, sl], s_probs[:, sl], s_v1[:, tt:tt + 1], None, ALU.is_lt), after=(a4,))
                a6 = step(lambda sl=sl: v.tensor_tensor(s_p2[:, sl], s_probs[:, sl], s_c1[:, sl], ALU.mult), after=(a5,))
                a7 = step(lambda sl=sl, tt=tt: v.reduce_max(s_v2[:, tt:tt + 1], s_p2[:, sl], axis=AXL.X), after=(a6,))
                a8 = step(lambda sl=sl, tt=tt: v.tensor_scalar(s_gmask[:, sl], s_probs[:, sl], s_v2[:, tt:tt + 1], None, ALU.is_ge), after=(a7,))
                a9 = step(lambda sl=sl: v.tensor_tensor(s_gates[:, sl], s_probs[:, sl], s_gmask[:, sl], ALU.mult), after=(a8,))
                step(lambda sl=sl: v.tensor_copy(s_gatesb[:, sl], s_gates[:, sl]), ("gatesb", grp, tt), after=(a9,))
            for e in range(NE):
                if e > 0:
                    em.wait(SA, ("gbev", grp, e - 1))
                    for f in range(32):
                        em.wait(SA, ("gelu", grp, e, f))
                        step(lambda e=e, f=f: v.tensor_tensor(
                            s_gh[:, f * GRP:(f + 1) * GRP], s_gtmp[f % 2][:],
                            ch(s_gb, e - 1), ALU.mult), ("gh", grp, e, f))
                em.wait(SP, ("mm2end", grp, e))
                if e == 0:
                    if grp > 0:
                        em.wait(SA, ("silu", grp - 1, 7))
                else:
                    em.wait(SV, ("yev", grp, e - 1, 7))
                for dp in range(8):
                    if e == 0:
                        step(lambda dp=dp: v.tensor_copy(ch(s_acc, dp), banks[dp][:]),
                             ("yev", grp, e, dp))
                    else:
                        step(lambda dp=dp: v.tensor_tensor(
                            ch(s_acc, dp), ch(s_acc, dp), banks[dp][:], ALU.add),
                            ("yev", grp, e, dp))
            em.wait(SP, ("lat", grp))
            l_bf = step(lambda: v.tensor_scalar(s_bitsf[:], banks[0][0:24, 0:GRP], s_nbel[:], None, ALU.is_gt), after=(l_nbel,))
            step(lambda: v.tensor_copy(s_bits[:], s_bitsf[:]), ("bits", grp), after=(l_bf,))
            em.wait(SP, ("syn", grp))
            l_sf = step(lambda: v.tensor_scalar(s_synpf[0:12, :], banks[1][0:12, 0:GRP], 0.5, None, ALU.is_gt), after=(l_syn0,))
            step(lambda: v.tensor_copy(s_synp[:], s_synpf[0:12, :]), ("synp", grp), after=(l_sf,))
            em.wait(SP, ("wsum", grp))
            l_f1 = step(lambda: v.tensor_scalar(s_ptmp[0:1, 0:GRP], banks[2][0:1, 0:GRP], 0.5, None, ALU.is_gt))
            l_f2 = step(lambda: v.tensor_scalar(s_ptmp[0:1, GRP:2 * GRP], banks[2][0:1, 0:GRP], 3.5, None, ALU.is_lt))
            step(lambda: v.tensor_tensor(s_fxb[:], s_ptmp[0:1, 0:GRP], s_ptmp[0:1, GRP:2 * GRP], ALU.mult), ("fix", grp), after=(l_f2,))
            em.wait(SP, ("fixbc", grp))
            l_rT = step(lambda: v.tensor_tensor(s_rT[:], s_synpf[:], banks[3][0:24, 0:GRP], ALU.mult))
            l_uT = step(lambda: v.tensor_tensor(s_uT[:], s_rT[:], s_bitsf[:], ALU.mult), after=(l_rT,))
            l_rT2 = step(lambda: v.tensor_tensor(s_rT[:], s_bitsf[:], s_rT[:], ALU.add), after=(l_uT,))
            step(lambda: v.scalar_tensor_tensor(s_corr[:], s_uT[:], -2.0, s_rT[:], ALU.mult, ALU.add), ("corr", grp), after=(l_rT2,))
            for dp in range(8):
                em.wait(SP, ("smm", grp, dp))
                l_t1 = step(lambda dp=dp: v.tensor_scalar(
                    s_t1[dp % 2][:], banks[4 + dp % 2][:], s_ble[:, dp:dp + 1],
                    None, ALU.add), ("t1", grp, dp),
                    after=(("sev", grp, dp - 2),) if dp >= 2 else ())
                step(lambda dp=dp: v.tensor_tensor(
                    ch(s_sT, dp), s_t1[dp % 2][:], ch(s_x1, dp), ALU.add),
                    ("sev", grp, dp), after=(l_t1,))
            for dp in range(8):
                em.wait(SA, ("silu", grp, dp))
                l_pr = step(lambda dp=dp: v.tensor_tensor(
                    s_pch[dp % 2][:], s_gch[dp % 2][:], ch(s_sT, dp), ALU.mult),
                    ("prod", grp, dp),
                    after=(("sev", grp, dp), ("out", grp, dp - 2))
                    if dp >= 2 else (("sev", grp, dp),))
                if grp > 0 or dp >= 2:
                    pv = grp * 8 + dp - 2
                    em.wait(SO[dp % 2], ("st", pv // 8, pv % 8))
                step(lambda grp=grp, dp=dp: v.tensor_tensor(
                    s_och[dp % 2][:], s_pch[dp % 2][:], chx(grp, dp), ALU.add),
                    ("out", grp, dp), after=(l_pr,))

    labels = {}
    streams = [
        ("sync", stream_sync), ("gpsimd", stream_gpsimd),
        ("tensor", stream_tensor), ("scalar", stream_scalar),
        ("vector", stream_vector),
    ]
    for _, fn in streams:
        fn(Em("count", labels))

    engmap = {"sync": blk.sync, "gpsimd": blk.gpsimd, "tensor": blk.tensor,
              "scalar": blk.scalar, "vector": blk.vector}
    for name, fn in streams:
        engmap[name](lambda eng, fn=fn: fn(Em("emit", labels, eng)))

    es.close()
    return nc


def _host_inputs(inputs):
    f32 = np.float32
    x = np.asarray(inputs["L0_attn_x"], dtype=f32)
    W1 = np.asarray(inputs["L0_ffn_W1"], dtype=f32)
    W2 = np.asarray(inputs["L0_ffn_W2"], dtype=f32)
    Ws1 = np.asarray(inputs["L0_ffn_Ws1"], dtype=f32)
    Ws2 = np.asarray(inputs["L0_ffn_Ws2"], dtype=f32)
    wr = np.asarray(inputs["L0_ffn_w_router"], dtype=f32)
    Wel = np.asarray(inputs["L0_attn_W_el"], dtype=f32)
    Wle = np.asarray(inputs["L0_attn_W_le"], dtype=f32)
    b1 = np.asarray(inputs["L0_ffn_b1"], dtype=f32)
    bs1 = np.asarray(inputs["L0_ffn_bs1"], dtype=f32)
    b2 = np.asarray(inputs["L0_ffn_b2"], dtype=f32)
    bs2 = np.asarray(inputs["L0_ffn_bs2"], dtype=f32)
    ble = np.asarray(inputs["L0_attn_b_le"], dtype=f32)
    bel = np.asarray(inputs["L0_attn_b_el"], dtype=f32)

    def c(a):
        return np.ascontiguousarray(a)

    W1TT = c(W1.transpose(0, 2, 1).reshape(8, 8, 128, 8, 512)
             .transpose(0, 3, 2, 1, 4).reshape(8, 8, 128, 4096))
    Ws1TT = c(Ws1.T.reshape(8, 128, 8, 512).transpose(2, 1, 0, 3)
              .reshape(8, 128, 4096))
    W2TT = c(W2.transpose(0, 2, 1).reshape(8, 8, 4, 128, 1024)
             .transpose(0, 1, 3, 2, 4).reshape(8, 8, 128, 4096))
    Ws2TT = c(Ws2.T.reshape(8, 4, 128, 1024).transpose(0, 2, 1, 3)
              .reshape(8, 128, 4096))
    wrT3 = c(wr.T.reshape(8, 128, 8))
    WelT3 = c(Wel.reshape(8, 128, 24))
    b1r = c(b1.reshape(8, 32, 128).transpose(0, 2, 1))
    bs1r = c(bs1.reshape(32, 128).T)
    bs2r = c(bs2.reshape(8, 128).T)
    bler = c(ble.reshape(8, 128).T)
    belr = c(bel.reshape(24, 1))

    _A = np.array([
        [1, 1, 1, 0, 1, 1, 1, 0, 0, 0, 1, 1], [1, 1, 0, 1, 1, 1, 0, 0, 0, 1, 1, 0],
        [1, 0, 1, 1, 1, 0, 0, 0, 1, 1, 0, 1], [0, 1, 1, 1, 0, 0, 0, 1, 1, 0, 1, 1],
        [0, 0, 0, 0, 1, 1, 1, 1, 1, 1, 1, 1], [1, 1, 1, 1, 0, 0, 0, 0, 0, 0, 0, 0],
        [0, 0, 0, 0, 1, 1, 1, 1, 0, 0, 0, 0], [1, 1, 0, 0, 1, 1, 0, 0, 1, 1, 0, 0],
        [1, 0, 1, 0, 1, 0, 1, 0, 1, 0, 1, 0], [0, 1, 0, 1, 0, 1, 0, 1, 0, 1, 0, 1],
        [1, 0, 0, 1, 1, 0, 0, 1, 1, 0, 0, 1], [0, 1, 1, 0, 0, 1, 1, 0, 0, 1, 1, 0]],
        dtype=f32)
    H = np.concatenate([_A.T, np.eye(12, dtype=f32)], axis=1)
    HT = c(H.T).astype(BF)

    ident = np.eye(128, dtype=f32).astype(BF)
    sel8 = np.zeros((8, 8 * 128), dtype=f32)
    for e in range(8):
        sel8[e, e * 128:(e + 1) * 128] = 1.0
    sel8 = sel8.astype(BF)

    k = np.fft.fftfreq(E).astype(np.float64)
    kav = (np.abs(k) + 1e-8).astype(f32)
    ka = c(kav.reshape(8, 128).T)
    ka2 = c((kav.astype(np.float64) ** 2).astype(f32).reshape(8, 128).T)
    lx = np.zeros((1, 8), dtype=f32)
    lx[0, :5] = np.log([2.0, 4.0, 8.0, 16.0, 32.0])
    lx[0, 5] = 1e-5

    kk = np.arange(E, dtype=np.float64)
    ang = 2.0 * np.pi * np.outer(kk, kk) / E
    Ck = np.cos(ang).astype(f32).reshape(8, 128, E).astype(BF)
    SkN = (-np.sin(ang)).astype(f32).reshape(8, 128, E).astype(BF)

    shared = {
        "W1TT": W1TT, "W2TT": W2TT, "Ws1TT": Ws1TT, "Ws2TT": Ws2TT,
        "wrT3": wrT3, "WelT3": WelT3, "Wle": c(Wle),
        "b1r": b1r, "bs1r": bs1r, "b2": c(b2), "bs2r": bs2r,
        "bler": bler, "bel": belr, "HT": HT, "ident": ident, "sel8": sel8,
        "onesf": np.stack([np.ones(128, dtype=f32), np.full(128, np.pi / 2, dtype=f32)], axis=1),
        "onesrf": np.ones((1, 128), dtype=f32),
        "onesrb": np.ones((1, 128), dtype=f32).astype(BF),
        "ones12b": np.ones((12, 1), dtype=f32).astype(BF),
        "ka": ka, "ka2": ka2, "lx": lx, "Ck": c(Ck), "SkN": c(SkN),
        "xfirst": c(x[0, 0].reshape(1, E)),
    }
    in_maps = []
    rows = B // NCORE
    for core in range(NCORE):
        xs = x[core * rows:(core + 1) * rows].reshape(TC, E)
        m = dict(shared)
        m["xT"] = c(xs.T)
        m["xTrev"] = c(xs.T[::-1])
        in_maps.append(m)
    return in_maps


def kernel(**inputs):
    in_maps = _host_inputs(inputs)
    nc = build_program()
    res = run_bass_kernel_spmd(nc, in_maps, core_ids=list(range(NCORE)))
    outs = []
    for core in range(NCORE):
        oT = res.results[core]["outT"]
        outs.append(np.ascontiguousarray(oT.T))
    return np.concatenate(outs, axis=0).reshape(B, S, E)


# version 09
# speedup vs baseline: 1.1489x; 1.1489x over previous
import math
from contextlib import ExitStack

import numpy as np
import ml_dtypes

import concourse.bass as bass
import concourse.mybir as mybir
from concourse.bass_utils import run_bass_kernel_spmd

F32 = mybir.dt.float32
BF16 = mybir.dt.bfloat16
AF = mybir.ActivationFunctionType
ALU = mybir.AluOpType
AXL = mybir.AxisListType
BF = ml_dtypes.bfloat16

B, S, E = 16, 2048, 1024
NCORE = 8
TC = B * S // NCORE
GRP = 512
NG = TC // GRP
NEXP = 8
NE = NEXP + 1
SCALE = 1.0 / math.sqrt(1.0 + 2.0 / 8.0)
EPS = 1e-5
PI = math.pi

_lx = np.log(np.array([2.0, 4.0, 8.0, 16.0, 32.0], dtype=np.float64))
SLX = float(_lx.sum())
DEN = float(5.0 * (_lx * _lx).sum() - SLX * SLX)


class Sem:
    def __init__(self, name):
        self.name = name
        self.h = None


class Em:

    def __init__(self, mode, labels, eng=None):
        self.mode = mode
        self.labels = labels
        self.counters = {}
        self.eng = eng
        self._last = None

    def do(self, fn):
        if self.mode == "emit":
            self._last = fn()

    def inc(self, sem, label=None, amt=1):
        v = self.counters.get(sem.name, 0) + amt
        self.counters[sem.name] = v
        if label is not None:
            key = (sem.name, label)
            if self.mode == "count":
                assert key not in self.labels, f"dup label {key}"
                self.labels[key] = v
        if self.mode == "emit":
            self._last.then_inc(sem.h, amt)

    def alias(self, sem, label):
        v = self.counters.get(sem.name, 0)
        if self.mode == "count":
            key = (sem.name, label)
            assert key not in self.labels, f"dup label {key}"
            self.labels[key] = v

    def wait(self, sem, label, delta=0):
        if self.mode == "emit":
            self.eng.wait_ge(sem.h, self.labels[(sem.name, label)] + delta)

    def wait_val(self, sem, v):
        if self.mode == "emit":
            self.eng.wait_ge(sem.h, v)


def build_program():
    nc = bass.Bass("TRN2", target_bir_lowering=False)

    d_xT = nc.dram_tensor("xT", [E, TC], F32, kind="ExternalInput").ap()
    d_xTrev = nc.dram_tensor("xTrev", [E, TC], F32, kind="ExternalInput").ap()
    d_xfirst = nc.dram_tensor("xfirst", [1, E], F32, kind="ExternalInput").ap()
    d_w1 = nc.dram_tensor("W1TT", [NEXP, 8, 128, 4096], F32, kind="ExternalInput").ap()
    d_w2 = nc.dram_tensor("W2TT", [NEXP, 8, 128, 4096], F32, kind="ExternalInput").ap()
    d_ws1 = nc.dram_tensor("Ws1TT", [8, 128, 4096], F32, kind="ExternalInput").ap()
    d_ws2 = nc.dram_tensor("Ws2TT", [8, 128, 4096], F32, kind="ExternalInput").ap()
    d_wr = nc.dram_tensor("wrT3", [8, 128, 8], F32, kind="ExternalInput").ap()
    d_wel = nc.dram_tensor("WelT3", [8, 128, 24], F32, kind="ExternalInput").ap()
    d_wle = nc.dram_tensor("Wle", [24, E], F32, kind="ExternalInput").ap()
    d_b1 = nc.dram_tensor("b1r", [NEXP, 128, 32], F32, kind="ExternalInput").ap()
    d_bs1 = nc.dram_tensor("bs1r", [128, 32], F32, kind="ExternalInput").ap()
    d_b2 = nc.dram_tensor("b2", [8, E], F32, kind="ExternalInput").ap()
    d_bs2 = nc.dram_tensor("bs2r", [128, 8], F32, kind="ExternalInput").ap()
    d_ble = nc.dram_tensor("bler", [128, 8], F32, kind="ExternalInput").ap()
    d_bel = nc.dram_tensor("bel", [24, 1], F32, kind="ExternalInput").ap()
    d_ht = nc.dram_tensor("HT", [24, 12], BF16, kind="ExternalInput").ap()
    d_ident = nc.dram_tensor("ident", [128, 128], BF16, kind="ExternalInput").ap()
    d_sel = nc.dram_tensor("sel8", [8, 8 * 128], BF16, kind="ExternalInput").ap()
    d_ones = nc.dram_tensor("onesf", [128, 2], F32, kind="ExternalInput").ap()
    d_onesr = nc.dram_tensor("onesrf", [1, 128], F32, kind="ExternalInput").ap()
    d_onesb = nc.dram_tensor("onesrb", [1, 128], BF16, kind="ExternalInput").ap()
    d_ones12 = nc.dram_tensor("ones12b", [12, 1], BF16, kind="ExternalInput").ap()
    d_ka = nc.dram_tensor("ka", [128, 8], F32, kind="ExternalInput").ap()
    d_ka2 = nc.dram_tensor("ka2", [128, 8], F32, kind="ExternalInput").ap()
    d_lx = nc.dram_tensor("lx", [1, 8], F32, kind="ExternalInput").ap()
    d_ck = nc.dram_tensor("Ck", [8, 128, 1024], BF16, kind="ExternalInput").ap()
    d_skn = nc.dram_tensor("SkN", [8, 128, 1024], BF16, kind="ExternalInput").ap()
    d_out = nc.dram_tensor("outT", [E, TC], F32, kind="ExternalOutput").ap()

    d_w1bf = nc.dram_tensor("w1bf", [NEXP, 8, 128, 4096], BF16, kind="Internal").ap()
    d_w2bf = nc.dram_tensor("w2bf", [NEXP, 8, 128, 4096], BF16, kind="Internal").ap()
    d_ws1bf = nc.dram_tensor("ws1bf", [8, 128, 4096], BF16, kind="Internal").ap()
    d_ws2bf = nc.dram_tensor("ws2bf", [8, 128, 4096], BF16, kind="Internal").ap()
    d_hh = nc.dram_tensor("hh", [2048], BF16, kind="Internal").ap()

    es = ExitStack()
    E_ = es.enter_context

    s_xT = [E_(nc.sbuf_tensor(f"s_xT{i}", [128, 8 * GRP], F32)) for i in range(2)]
    s_xnT = E_(nc.sbuf_tensor("s_xnT", [128, 8 * GRP], BF16))
    s_xnTr = E_(nc.sbuf_tensor("s_xnTr", [128, 8 * GRP], BF16))
    s_acc = E_(nc.sbuf_tensor("s_acc", [128, 8 * GRP], F32))
    s_gh = E_(nc.sbuf_tensor("s_gh", [128, 32 * GRP], BF16))
    s_gb = E_(nc.sbuf_tensor("s_gb", [128, 8 * GRP], BF16))
    s_w1b = [E_(nc.sbuf_tensor(f"s_w1b{i}", [128, 8 * 512], BF16)) for i in range(2)]
    s_w2b = [E_(nc.sbuf_tensor(f"s_w2b{i}", [128, 4 * 1024], BF16)) for i in range(2)]
    s_msb = [E_(nc.sbuf_tensor(f"s_msb{i}", [128, 1024], BF16)) for i in range(2)]
    s_x1 = E_(nc.sbuf_tensor("s_x1", [128, 8 * GRP], BF16))
    s_sT = E_(nc.sbuf_tensor("s_sT", [128, 8 * GRP], BF16))
    s_gtmp = [E_(nc.sbuf_tensor(f"s_gtmp{i}", [128, GRP], BF16)) for i in range(2)]
    s_sq = [E_(nc.sbuf_tensor(f"s_sq{i}", [128, GRP], F32)) for i in range(2)]
    s_xtmp = [E_(nc.sbuf_tensor(f"s_xtmp{i}", [128, GRP], F32)) for i in range(2)]
    s_gch = [E_(nc.sbuf_tensor(f"s_gch{i}", [128, GRP], BF16)) for i in range(2)]
    s_pch = [E_(nc.sbuf_tensor(f"s_pch{i}", [128, GRP], F32)) for i in range(2)]
    s_och = [E_(nc.sbuf_tensor(f"s_och{i}", [128, GRP], F32)) for i in range(2)]
    s_t1 = [E_(nc.sbuf_tensor(f"s_t1_{i}", [128, GRP], BF16)) for i in range(2)]

    s_wr = E_(nc.sbuf_tensor("s_wr", [128, 64], BF16))
    s_wel = E_(nc.sbuf_tensor("s_wel", [128, 8 * 24], BF16))
    s_wle = E_(nc.sbuf_tensor("s_wle", [24, E], BF16))
    s_b1 = E_(nc.sbuf_tensor("s_b1", [128, NEXP * 32], F32))
    s_bs1 = E_(nc.sbuf_tensor("s_bs1", [128, 32], F32))
    s_b2 = E_(nc.sbuf_tensor("s_b2", [8, E], BF16))
    s_bs2 = E_(nc.sbuf_tensor("s_bs2", [128, 8], F32))
    s_bs2s = E_(nc.sbuf_tensor("s_bs2s", [128, 8], F32))
    s_ble = E_(nc.sbuf_tensor("s_ble", [128, 8], F32))
    s_bel = E_(nc.sbuf_tensor("s_bel", [24, 1], F32))
    s_nbel = E_(nc.sbuf_tensor("s_nbel", [24, 1], F32))
    s_ht = E_(nc.sbuf_tensor("s_ht", [24, 12], BF16))
    s_id = E_(nc.sbuf_tensor("s_id", [128, 128], BF16))
    s_sel = E_(nc.sbuf_tensor("s_sel", [8, 8 * 128], BF16))
    s_onesc = E_(nc.sbuf_tensor("s_onesc", [128, 2], F32))
    s_onesr = E_(nc.sbuf_tensor("s_onesr", [1, 128], F32))
    s_onesb = E_(nc.sbuf_tensor("s_onesb", [1, 128], BF16))
    s_ones12 = E_(nc.sbuf_tensor("s_ones12", [12, 1], BF16))
    s_ka = E_(nc.sbuf_tensor("s_ka", [128, 8], F32))
    s_ka2 = E_(nc.sbuf_tensor("s_ka2", [128, 8], F32))
    s_lx = E_(nc.sbuf_tensor("s_lx", [1, 8], F32))

    s_nmax = E_(nc.sbuf_tensor("s_nmax", [128, 4], F32))
    s_esm = E_(nc.sbuf_tensor("s_esm", [128, 32], F32))
    s_ssum = E_(nc.sbuf_tensor("s_ssum", [128, 4], F32))
    s_srec = E_(nc.sbuf_tensor("s_srec", [128, 4], F32))
    s_probs = E_(nc.sbuf_tensor("s_probs", [128, 32], F32))
    s_v1 = E_(nc.sbuf_tensor("s_v1", [128, 4], F32))
    s_c1 = E_(nc.sbuf_tensor("s_c1", [128, 32], F32))
    s_p2 = E_(nc.sbuf_tensor("s_p2", [128, 32], F32))
    s_v2 = E_(nc.sbuf_tensor("s_v2", [128, 4], F32))
    s_gmask = E_(nc.sbuf_tensor("s_gmask", [128, 32], F32))
    s_gates = E_(nc.sbuf_tensor("s_gates", [128, 32], F32))
    s_gatesb = E_(nc.sbuf_tensor("s_gatesb", [128, 32], BF16))
    s_gT = E_(nc.sbuf_tensor("s_gT", [8, GRP], BF16))
    s_bitsf = E_(nc.sbuf_tensor("s_bitsf", [24, GRP], F32))
    s_bits = E_(nc.sbuf_tensor("s_bits", [24, GRP], BF16))
    s_synpf = E_(nc.sbuf_tensor("s_synpf", [24, GRP], F32))
    s_synp = E_(nc.sbuf_tensor("s_synp", [12, GRP], BF16))
    s_fxb = E_(nc.sbuf_tensor("s_fxb", [1, GRP], BF16))
    s_rT = E_(nc.sbuf_tensor("s_rT", [24, GRP], F32))
    s_uT = E_(nc.sbuf_tensor("s_uT", [24, GRP], F32))
    s_corr = E_(nc.sbuf_tensor("s_corr", [24, GRP], BF16))

    s_cnt = E_(nc.sbuf_tensor("s_cnt", [1, 8], F32))
    s_cnl = E_(nc.sbuf_tensor("s_cnl", [1, 8], F32))
    s_ptmp = E_(nc.sbuf_tensor("s_ptmp", [1, E], F32))
    s_dsrow = E_(nc.sbuf_tensor("s_dsrow", [1, 4], F32))
    s_bcD = E_(nc.sbuf_tensor("s_bcD", [128, 4], F32))
    s_ang = E_(nc.sbuf_tensor("s_ang", [128, 8], F32))
    s_ph = E_(nc.sbuf_tensor("s_ph", [128, 8], F32))
    s_mgt = E_(nc.sbuf_tensor("s_mgt", [128, 8], F32))
    s_zz = E_(nc.sbuf_tensor("s_zz", [128, 8], F32))
    s_sgn = E_(nc.sbuf_tensor("s_sgn", [128, 8], F32))
    s_sa = E_(nc.sbuf_tensor("s_sa", [128, 8], F32))
    s_sing = E_(nc.sbuf_tensor("s_sing", [128, 8], F32))
    s_cosp = E_(nc.sbuf_tensor("s_cosp", [128, 8], F32))
    s_sinp = E_(nc.sbuf_tensor("s_sinp", [128, 8], F32))
    s_rf = E_(nc.sbuf_tensor("s_rf", [128, 8], F32))
    s_if = E_(nc.sbuf_tensor("s_if", [128, 8], F32))
    s_rfb = E_(nc.sbuf_tensor("s_rfb", [128, 8], BF16))
    s_ifb = E_(nc.sbuf_tensor("s_ifb", [128, 8], BF16))
    s_hreb = E_(nc.sbuf_tensor("s_hreb", [1, E], BF16))

    banks = [E_(nc.psum_tensor(f"bank{i}", [128, 512], F32)) for i in range(8)]

    s_xn00 = s_acc[0:1, 0:1024]
    s_dsc = s_acc[0:1, 1536:1568]
    s_mrow = s_acc[0:1, 0:512]
    s_vrow = s_acc[0:1, 512:1024]
    s_rrow = s_acc[0:1, 1024:1536]

    SP = Sem("SP"); SA = Sem("SA"); SV = Sem("SV")
    PC = Sem("PC"); PK = Sem("PK"); PH = Sem("PH")
    PT = [Sem("PTa"), Sem("PTb")]
    PR = [Sem("PRa"), Sem("PRb")]
    PM = [Sem("PMa"), Sem("PMb")]
    SO = [Sem("SOa"), Sem("SOb")]
    PWC = Sem("PWC")
    PW1 = [Sem("PW1a"), Sem("PW1b")]
    PW2 = [Sem("PW2a"), Sem("PW2b")]
    PS2 = [Sem("PS2a"), Sem("PS2b")]
    PB1 = [Sem("PB1a"), Sem("PB1b")]
    PB2 = [Sem("PB2a"), Sem("PB2b")]
    for s in (SP, SA, SV, PC, PK, PH, *PT, *PR, *PM, *SO, PWC, *PW1, *PW2,
              *PS2, *PB1, *PB2):
        s.h = es.enter_context(nc.semaphore(s.name))

    blk = es.enter_context(nc.Block())

    def ch(t, dc, w=GRP):
        return t[:, dc * w:(dc + 1) * w]

    def chx(grp, dc):
        return s_xT[grp % 2][:, dc * GRP:(dc + 1) * GRP]

    NOUT_DMAS = NG * 8

    def stream_sync(em):
        g = em.eng
        em.do(lambda: g.dma_start(s_onesc[:], d_ones)); em.inc(PC, None, 16)
        em.do(lambda: g.dma_start(s_onesr[:], d_onesr)); em.inc(PC, None, 16)
        em.do(lambda: g.dma_start(s_onesb[:], d_onesb)); em.inc(PC, None, 16)
        em.do(lambda: g.dma_start(s_ones12[:], d_ones12)); em.inc(PC, None, 16)
        em.do(lambda: g.dma_start(s_ka[:], d_ka)); em.inc(PC, None, 16)
        em.do(lambda: g.dma_start(s_ka2[:], d_ka2)); em.inc(PC, None, 16)
        em.do(lambda: g.dma_start(s_lx[:], d_lx)); em.inc(PC, None, 16)
        em.do(lambda: g.dma_start(s_ht[:], d_ht)); em.inc(PC, None, 16)
        em.do(lambda: g.dma_start(s_id[:], d_ident)); em.inc(PC, None, 16)
        em.do(lambda: g.dma_start(s_sel[:], d_sel)); em.inc(PC, None, 16)
        em.do(lambda: g.dma_start(
            s_b1[:].rearrange("p (e j) -> p e j", e=NEXP),
            d_b1.rearrange("e p j -> p e j")))
        em.inc(PC, None, 16)
        em.do(lambda: g.dma_start(s_bs1[:], d_bs1)); em.inc(PC, None, 16)
        em.do(lambda: g.dma_start(s_bs2[:], d_bs2)); em.inc(PC, None, 16)
        em.do(lambda: g.dma_start(s_ble[:], d_ble)); em.inc(PC, None, 16)
        em.do(lambda: g.dma_start(s_bel[:], d_bel)); em.inc(PC, None, 16)
        em.do(lambda: g.dma_start(s_ptmp[:], d_xfirst)); em.inc(PC, ("consts",), 16)
        for c in range(8):
            if c > 0:
                em.wait(SP, ("hre", c - 1))
            em.do(lambda c=c: g.dma_start(s_msb[0][:], d_ck[c]))
            em.inc(PK, None, 16)
            em.do(lambda c=c: g.dma_start(s_msb[1][:], d_skn[c]))
            em.inc(PK, ("ckskn", c), 16)
        em.wait(SA, ("hrev", 1))
        em.do(lambda: g.dma_start(bass.AP(d_hh.tensor, 0, [[1, 1], [1, 1024]]), s_hreb[0:1, :]))
        em.inc(PH, None, 16)
        em.do(lambda: g.dma_start(bass.AP(d_hh.tensor, 1024, [[1, 1], [1, 1024]]), s_hreb[0:1, :]))
        em.inc(PH, ("hh",), 16)

        for grp in range(2):
            for dc in range(8):
                em.do(lambda grp=grp, dc=dc: g.dma_start(
                    s_xT[grp % 2][:, dc * GRP:(dc + 1) * GRP],
                    d_xT[dc * 128:(dc + 1) * 128, grp * GRP:(grp + 1) * GRP]))
                em.inc(PT[grp % 2], ("xTg", grp) if dc == 7 else None, 16)

        for grp in range(NG):
            for dc in range(8):
                if dc < 2:
                    em.wait(SP, ("sqsum", grp, 7))
                else:
                    em.wait(SV, ("xnr", grp, dc - 2))
                em.do(lambda grp=grp, dc=dc: g.dma_start(
                    s_sq[dc % 2][:],
                    d_xTrev[dc * 128:(dc + 1) * 128, grp * GRP:(grp + 1) * GRP]))
                em.inc(PR[dc % 2], ("xTr", grp, dc), 16)
            if grp == 0:
                for e in range(NE):
                    for fb in range(8):
                        k = e * 8 + fb
                        em.wait(PW1[k % 2], ("w1", 0, e, fb))
                        dst = d_ws1bf[fb] if e == 0 else d_w1bf[e - 1][fb]
                        em.do(lambda dst=dst, b=(k + 1) % 2: g.dma_start(
                            dst, s_w1b[b][:]))
                        em.inc(PB1[k % 2], ("wb1", k), 16)
                    for pk in range(8):
                        k = e * 8 + pk
                        em.wait(PW2[k % 2], ("w2", 0, e, pk))
                        dst = d_ws2bf[pk] if e == 0 else d_w2bf[e - 1][pk]
                        em.do(lambda dst=dst, b=(k + 1) % 2: g.dma_start(
                            dst, s_w2b[b][:]))
                        em.inc(PB2[k % 2], ("wb2", k), 16)
            for m in range(8):
                if grp == 0 and m < 2:
                    em.wait(PH, ("hh",))
                    em.wait(SP, ("hre", 7))
                else:
                    pm = (grp * 8 + m) - 2
                    em.wait(SP, ("convm", pm // 8, pm % 8))
                em.do(lambda m=m: g.dma_start(
                    s_msb[m % 2][:],
                    bass.AP(d_hh.tensor, m * 128 + 1, [[1, 128], [1, 1024]])))
                em.inc(PM[m % 2], ("ms", grp, m), 16)
            for dp in range(8):
                em.wait(SV, ("out", grp, dp))
                em.do(lambda grp=grp, dp=dp: g.dma_start(
                    d_out[dp * 128:(dp + 1) * 128, grp * GRP:(grp + 1) * GRP],
                    s_och[dp % 2][:]))
                em.inc(SO[dp % 2], ("st", grp, dp), 16)
            if grp + 2 < NG:
                for dc in range(8):
                    em.do(lambda grp=grp, dc=dc: g.dma_start(
                        s_xT[grp % 2][:, dc * GRP:(dc + 1) * GRP],
                        d_xT[dc * 128:(dc + 1) * 128,
                             (grp + 2) * GRP:(grp + 3) * GRP]))
                    em.inc(PT[grp % 2], ("xTg", grp + 2) if dc == 7 else None, 16)
        em.wait_val(SO[0], 16 * (NOUT_DMAS // 2))
        em.wait_val(SO[1], 16 * (NOUT_DMAS // 2))

    def stream_gpsimd(em):
        g = em.eng
        em.do(lambda: g.dma_start(
            s_wr[:].rearrange("p (c j) -> p c j", c=8),
            d_wr.rearrange("c p j -> p c j")))
        em.inc(PWC, None, 16)
        em.do(lambda: g.dma_start(
            s_wel[:].rearrange("p (c j) -> p c j", c=8),
            d_wel.rearrange("c p j -> p c j")))
        em.inc(PWC, None, 16)
        em.do(lambda: g.dma_start(s_wle[:], d_wle)); em.inc(PWC, None, 16)
        em.do(lambda: g.dma_start(s_b2[:], d_b2)); em.inc(PWC, ("smallcast",), 16)

        wslab = 0
        wpack = 0
        for grp in range(NG):
            for e in range(NE):
                for fb in range(8):
                    wslab += 1
                    if wslab > 2:
                        i2 = wslab - 2
                        g2, r2 = divmod(i2 - 1, NE * 8)
                        e2, fb2 = divmod(r2, 8)
                        em.wait(SP, ("mm1", g2, e2, fb2 * 4 + 3))
                        if i2 <= NE * 8:
                            em.wait(PB1[(i2 - 1) % 2], ("wb1", i2 - 1))
                    if grp == 0:
                        w1src = d_ws1 if e == 0 else d_w1[e - 1]
                    else:
                        w1src = d_ws1bf if e == 0 else d_w1bf[e - 1]
                    em.do(lambda w1src=w1src, fb=fb, b=wslab % 2: g.dma_start(
                        s_w1b[b][:], w1src[fb]))
                    em.inc(PW1[(wslab - 1) % 2], ("w1", grp, e, fb), 16)
                if grp == 0:
                    for pk in range(8):
                        wpack += 1
                        if wpack > 2:
                            j2 = wpack - 2
                            e2, pk2 = divmod(j2 - 1, 8)
                            em.wait(SP, ("mm2p", 0, e2, pk2))
                            em.wait(PB2[(j2 - 1) % 2], ("wb2", j2 - 1))
                        w2src = d_ws2 if e == 0 else d_w2[e - 1]
                        em.do(lambda w2src=w2src, pk=pk, b=wpack % 2: g.dma_start(
                            s_w2b[b][:], w2src[pk]))
                        em.inc(PW2[(wpack - 1) % 2], ("w2", 0, e, pk), 16)

    def stream_tensor(em):
        t = em.eng
        bfree = {i: None for i in range(8)}

        def use_bank(i):
            if bfree[i] is not None:
                em.wait(*bfree[i])
                bfree[i] = None

        em.wait(SV, ("dsrow",))
        em.do(lambda: t.matmul(banks[2][0:128, 0:4], s_onesr[:], s_dsrow[:],
                               start=True, stop=True))
        em.inc(SP, ("bcD",), 1)
        em.wait(SV, ("ifb",))
        for c in range(8):
            em.wait(PK, ("ckskn", c))
            for half in range(2):
                em.do(lambda c=c, half=half: t.matmul(
                    banks[half][0:1, 0:512], s_rfb[:, c:c + 1],
                    s_msb[0][:, half * 512:(half + 1) * 512],
                    start=(c == 0), stop=False))
                em.do(lambda c=c, half=half: t.matmul(
                    banks[half][0:1, 0:512], s_ifb[:, c:c + 1],
                    s_msb[1][:, half * 512:(half + 1) * 512],
                    start=False, stop=(c == 7)))
            em.inc(SP, ("hre", c), 1)
        bfree[0] = (SA, ("hrev", 0))
        bfree[1] = (SA, ("hrev", 1))
        bfree[2] = (SA, ("bcDev",))
        em.wait(PWC, ("smallcast",))

        for grp in range(NG):
            use_bank(6)
            em.wait(PT[grp % 2], ("xTg", grp))
            for dc in range(8):
                em.do(lambda grp=grp, dc=dc: t.matmul(
                    banks[6][0:1, 0:GRP], s_onesc[:, 0:1], chx(grp, dc),
                    start=(dc == 0), stop=(dc == 7)))
            em.inc(SP, ("msum", grp), 1)
            use_bank(7)
            for dc in range(8):
                em.wait(SA, ("sq", grp, dc))
                em.do(lambda dc=dc: t.matmul(
                    banks[7][0:1, 0:GRP], s_onesc[:, 0:1], s_sq[dc % 2][:],
                    start=(dc == 0), stop=(dc == 7)))
                em.inc(SP, ("sqsum", grp, dc), 1)
            em.wait(SV, ("mmean", grp))
            em.do(lambda: t.matmul(banks[6][:, 0:GRP], s_onesr[:], s_mrow[:],
                                   start=True, stop=True))
            em.inc(SP, ("mb", grp), 1)
            em.wait(SA, ("rstd", grp))
            em.do(lambda: t.matmul(banks[7][:, 0:GRP], s_onesr[:], s_rrow[:],
                                   start=True, stop=True))
            em.inc(SP, ("rb", grp), 1)
            bfree[6] = (SV, ("xnr", grp, 7))
            bfree[7] = (SV, ("xnr", grp, 7))
            use_bank(5)
            for tt in range(4):
                for dc in range(8):
                    if tt == 0:
                        em.wait(SV, ("xn", grp, dc))
                    em.do(lambda tt=tt, dc=dc: t.matmul(
                        banks[5][0:128, tt * 128:tt * 128 + 8],
                        s_xnT[:, dc * GRP + tt * 128: dc * GRP + (tt + 1) * 128],
                        s_wr[:, dc * 8:(dc + 1) * 8],
                        start=(dc == 0), stop=(dc == 7)))
                em.inc(SP, ("logits", grp, tt), 1)
            bfree[5] = (SA, ("exp", grp, 3))

            for e in range(NE):
                w1buf = None
                for f in range(32):
                    fb, k = divmod(f, 4)
                    if k == 0:
                        em.wait(PW1[(grp * NE * 8 + e * 8 + fb) % 2],
                                ("w1", grp, e, fb))
                        w1buf = s_w1b[(grp * NE * 8 + e * 8 + fb + 1) % 2]
                    use_bank(f % 4)
                    for dc in range(8):
                        em.do(lambda w1buf=w1buf, f=f, dc=dc, k=k: t.matmul(
                            banks[f % 4][:],
                            w1buf[:, dc * 512 + k * 128: dc * 512 + (k + 1) * 128],
                            ch(s_xnT, dc),
                            start=(dc == 0), stop=(dc == 7)))
                    em.inc(SP, ("mm1", grp, e, f), 1)
                    bfree[f % 4] = (SA, ("gelu", grp, e, f))
                if e == 0:
                    use_bank(4)
                    for tt in range(4):
                        em.wait(SV, ("gatesb", grp, tt))
                        em.do(lambda tt=tt: t.transpose(
                            banks[4][0:128, 0:256].bitcast(BF16)[0:8, tt * 128:(tt + 1) * 128],
                            s_gatesb[:, tt * 8:(tt + 1) * 8], s_id[:]))
                        em.inc(SP, ("gtr", grp, tt), 1)
                    bfree[4] = (SA, ("gTev", grp))
                    use_bank(3)
                    for ge in range(8):
                        if ge == 0:
                            em.wait(SA, ("gTev", grp))
                        else:
                            em.wait(SA, ("gbev", grp, ge - 1))
                        em.do(lambda ge=ge: t.matmul(
                            banks[3][:, 0:GRP], s_sel[:, ge * 128:(ge + 1) * 128],
                            s_gT[:], start=True, stop=True))
                        em.inc(SP, ("gbc", grp, ge), 1)
                    bfree[3] = (SA, ("gbev", grp, 7))
                for i in range(8):
                    use_bank(i)
                w2buf = None
                for f in range(32):
                    pk, j = divmod(f, 4)
                    if j == 0:
                        idx = grp * NE * 8 + e * 8 + pk
                        em.wait(PW2[idx % 2] if grp == 0 else PS2[idx % 2],
                                ("w2", grp, e, pk))
                        w2buf = s_w2b[(idx + 1) % 2]
                    if e == 0:
                        em.wait(SA, ("gelu", grp, 0, f))
                    else:
                        em.wait(SV, ("gh", grp, e, f))
                    for dp in range(8):
                        em.do(lambda w2buf=w2buf, j=j, dp=dp, f=f, e=e: t.matmul(
                            banks[dp][:],
                            w2buf[:, j * 1024 + dp * 128: j * 1024 + (dp + 1) * 128],
                            s_gh[:, f * GRP:(f + 1) * GRP],
                            start=(f == 0),
                            stop=(f == 31 and e != NE - 1)))
                    if j == 3:
                        em.inc(SP, ("mm2p", grp, e, pk), 1)
                if e == NE - 1:
                    for dp in range(8):
                        em.do(lambda dp=dp: t.matmul(
                            banks[dp][:], s_b2[:, dp * 128:(dp + 1) * 128],
                            s_gT[:], start=False, stop=True))
                    em.inc(SP, ("mm2end", grp, e), 1)
                else:
                    em.alias(SP, ("mm2end", grp, e))
                for i in range(8):
                    bfree[i] = (SV, ("yev", grp, e, i))

            for i in range(8):
                use_bank(i)
            for m in range(8):
                em.wait(PM[m % 2], ("ms", grp, m))
                em.wait(SV, ("xnr", grp, m))
                for n in range(8):
                    em.do(lambda m=m, n=n: t.matmul(
                        banks[n][:], s_msb[m % 2][:, n * 128:(n + 1) * 128],
                        ch(s_xnTr, m),
                        start=(m == 0), stop=(m == 7)))
                em.inc(SP, ("convm", grp, m), 1)
            for n in range(8):
                bfree[n] = (SA, ("x1", grp, n))
            use_bank(0)
            for dc in range(8):
                em.wait(SA, ("x1", grp, dc))
                em.do(lambda dc=dc: t.matmul(
                    banks[0][0:24, 0:GRP], s_wel[:, dc * 24:(dc + 1) * 24],
                    ch(s_x1, dc), start=(dc == 0), stop=(dc == 7)))
            em.inc(SP, ("lat", grp), 1)
            bfree[0] = (SV, ("bits", grp))
            use_bank(1)
            em.wait(SV, ("bits", grp))
            em.do(lambda: t.matmul(banks[1][0:12, 0:GRP], s_ht[:], s_bits[:],
                                   start=True, stop=True))
            em.inc(SP, ("syn", grp), 1)
            bfree[1] = (SV, ("synp", grp))
            use_bank(2)
            em.wait(SV, ("synp", grp))
            em.do(lambda: t.matmul(banks[2][0:1, 0:GRP], s_ones12[:],
                                   s_synp[:], start=True, stop=True))
            em.inc(SP, ("wsum", grp), 1)
            bfree[2] = (SV, ("fix", grp))
            use_bank(3)
            em.wait(SV, ("fix", grp))
            em.do(lambda: t.matmul(banks[3][0:24, 0:GRP], s_onesb[0:1, 0:24],
                                   s_fxb[:], start=True, stop=True))
            em.inc(SP, ("fixbc", grp), 1)
            bfree[3] = (SV, ("corr", grp))
            em.wait(SV, ("corr", grp))
            for dp in range(8):
                use_bank(4 + dp % 2)
                em.do(lambda dp=dp: t.matmul(
                    banks[4 + dp % 2][:], s_wle[:, dp * 128:(dp + 1) * 128],
                    s_corr[:], start=True, stop=True))
                em.inc(SP, ("smm", grp, dp), 1)
                bfree[4 + dp % 2] = (SV, ("t1", grp, dp))

    def stream_scalar(em):
        sc = em.eng
        em.wait(PC, ("consts",))
        em.do(lambda: sc.activation(s_xn00[:], s_ptmp[:], AF.Square,
                                    accum_out=s_dsc[0:1, 1:2]))
        em.inc(SA, ("xfsq",), 1)
        em.wait(SV, ("cnt",))
        em.do(lambda: sc.activation(s_cnl[0:1, 0:5], s_cnt[0:1, 0:5], AF.Ln))
        em.inc(SA, ("cntln",), 1)
        em.wait(SP, ("bcD",))
        em.do(lambda: sc.activation(s_bcD[:], banks[2][0:128, 0:4], AF.Copy))
        em.inc(SA, ("bcDev",), 1)
        em.wait(SV, ("zz",))
        em.do(lambda: sc.activation(s_sa[:], s_zz[:], AF.Sin))
        em.inc(SA, ("sinzz",), 1)
        em.wait(SV, ("ph",))
        em.do(lambda: sc.activation(s_cosp[:], s_ph[:], AF.Sin, scale=-1.0, bias=s_onesc[:, 1:2]))
        em.inc(SA, ("cosp",), 1)
        em.do(lambda: sc.activation(s_sinp[:], s_ph[:], AF.Sin))
        em.inc(SA, ("sinp",), 1)
        em.wait(SP, ("hre", 7))
        for half in range(2):
            em.do(lambda half=half: sc.activation(
                s_hreb[0:1, half * 512:(half + 1) * 512], banks[half][0:1, 0:512],
                AF.Copy, scale=1.0 / E))
            em.inc(SA, ("hrev", half), 1)

        def w2load(grp, e, pk):
            idx = grp * NE * 8 + e * 8 + pk
            g2, r2 = divmod(idx - 2, NE * 8)
            e2, pk2 = divmod(r2, 8)
            em.wait(SP, ("mm2p", g2, e2, pk2))
            w2src = d_ws2bf if e == 0 else d_w2bf[e - 1]
            em.do(lambda w2src=w2src, pk=pk, b=(idx + 1) % 2: sc.dma_start(
                s_w2b[b][:], w2src[pk]))
            em.inc(PS2[idx % 2], ("w2", grp, e, pk), 16)

        gtmp_free = {0: None, 1: None}
        for grp in range(NG):
            for dc in range(8):
                if dc == 0:
                    em.wait(PT[grp % 2], ("xTg", grp))
                if dc >= 2:
                    em.wait(SP, ("sqsum", grp, dc - 2))
                elif grp > 0:
                    em.wait(SV, ("xnr", grp - 1, 6 + dc))
                em.do(lambda grp=grp, dc=dc: sc.activation(
                    s_sq[dc % 2][:], chx(grp, dc), AF.Square))
                em.inc(SA, ("sq", grp, dc), 1)
            if grp > 0:
                w2load(grp, 0, 0)
                w2load(grp, 0, 1)
            em.wait(SV, ("var", grp))
            em.do(lambda: sc.activation(s_vrow[:], s_vrow[:], AF.Ln, bias=s_lx[0:1, 5:6]))
            em.inc(SA, ("lnv", grp), 1)
            em.wait(SA, ("lnv", grp))
            em.do(lambda: sc.activation(s_rrow[:], s_vrow[:], AF.Exp, scale=-0.5))
            em.inc(SA, ("rstd", grp), 1)
            for tt in range(4):
                em.wait(SV, ("negmax", grp, tt))
                em.do(lambda tt=tt: sc.activation(
                    s_esm[:, tt * 8:(tt + 1) * 8],
                    banks[5][0:128, tt * 128:tt * 128 + 8],
                    AF.Exp, bias=s_nmax[:, tt:tt + 1]))
                em.inc(SA, ("exp", grp, tt), 1)
            for f in range(32):
                em.wait(SP, ("mm1", grp, 0, f))
                em.do(lambda f=f: sc.activation(
                    s_gh[:, f * GRP:(f + 1) * GRP], banks[f % 4][:],
                    AF.Gelu_apprx_tanh, bias=s_bs1[:, f:f + 1]))
                em.inc(SA, ("gelu", grp, 0, f), 1)
            for tt in range(4):
                em.wait(SP, ("gtr", grp, tt))
            em.do(lambda: sc.activation(
                s_gT[:], banks[4][0:128, 0:256].bitcast(BF16)[0:8, 0:512], AF.Copy))
            em.inc(SA, ("gTev", grp), 1)
            for ge in range(8):
                em.wait(SP, ("gbc", grp, ge))
                if grp > 0:
                    em.wait(SV, ("gh", grp - 1, NE - 1, 31))
                em.do(lambda ge=ge: sc.activation(
                    ch(s_gb, ge), banks[3][:, 0:GRP], AF.Copy))
                em.inc(SA, ("gbev", grp, ge), 1)
            if grp > 0:
                for pk in range(2, 8):
                    w2load(grp, 0, pk)
            for e in range(1, NE):
                for f in range(32):
                    em.wait(SP, ("mm1", grp, e, f))
                    if gtmp_free[f % 2] is not None:
                        em.wait(SV, gtmp_free[f % 2])
                    em.do(lambda e=e, f=f: sc.activation(
                        s_gtmp[f % 2][:], banks[f % 4][:],
                        AF.Gelu_apprx_tanh,
                        bias=s_b1[:, (e - 1) * 32 + f:(e - 1) * 32 + f + 1]))
                    em.inc(SA, ("gelu", grp, e, f), 1)
                    gtmp_free[f % 2] = ("gh", grp, e, f)
                    if grp > 0 and f == 0:
                        w2load(grp, e, 0)
                        w2load(grp, e, 1)
                if grp > 0:
                    for pk in range(2, 8):
                        w2load(grp, e, pk)
            em.wait(SP, ("convm", grp, 7))
            for n in range(8):
                em.do(lambda n=n: sc.activation(ch(s_x1, n), banks[n][:], AF.Copy))
                em.inc(SA, ("x1", grp, n), 1)
            for dp in range(8):
                em.wait(SV, ("yev", grp, NE - 1, dp))
                if grp > 0 or dp >= 2:
                    pv = grp * 8 + dp - 2
                    em.wait(SV, ("prod", pv // 8, pv % 8))
                em.do(lambda dp=dp: sc.activation(
                    s_gch[dp % 2][:], ch(s_acc, dp), AF.Silu, scale=SCALE,
                    bias=s_bs2s[:, dp:dp + 1]))
                em.inc(SA, ("silu", grp, dp), 1)

    def stream_vector(em):
        v = em.eng
        kc = [0]

        def step(fn, name=None, *, after=()):
            for w in after:
                em.wait(SV, w)
            em.do(fn)
            if name is None:
                kc[0] += 1
                name = ("v", kc[0])
            em.inc(SV, name, 1)
            return name

        em.wait(PC, ("consts",))
        step(lambda: v.tensor_scalar(s_bs2s[:], s_bs2[:], SCALE, None, ALU.mult))
        l_nbel = step(lambda: v.tensor_scalar(s_nbel[:], s_bel[:], -1.0, None, ALU.mult))
        l_syn0 = step(lambda: v.memset(s_synpf[:], 0.0), ("init",))
        l_sum = step(lambda: v.reduce_sum(s_dsc[0:1, 0:1], s_ptmp[:], axis=AXL.X))
        em.wait(SA, ("xfsq",))
        l_mean = step(lambda: v.tensor_scalar(s_dsc[0:1, 2:3], s_dsc[0:1, 0:1], 1.0 / E, None, ALU.mult), after=(l_sum,))
        l_xn00 = step(lambda: v.tensor_scalar(s_xn00[:], s_ptmp[:], s_dsc[0:1, 2:3], None, ALU.subtract), ("xn00",), after=(l_mean,))
        prev = l_xn00
        for i, n in enumerate([4, 16, 64, 256, 1024]):
            l_bm = step(lambda i=i, n=n: v.reduce_sum(s_dsc[0:1, 8 + i:9 + i], s_xn00[0:1, 0:n], axis=AXL.X), after=(prev,))
            l_bm2 = step(lambda i=i: v.tensor_scalar(s_dsc[0:1, 8 + i:9 + i], s_dsc[0:1, 8 + i:9 + i], 1.0 / n, None, ALU.mult), after=(l_bm,))
            l_gt = step(lambda i=i, n=n: v.tensor_scalar(s_ptmp[0:1, 0:n], s_xn00[0:1, 0:n], s_dsc[0:1, 8 + i:9 + i], None, ALU.is_gt), after=(l_bm2,))
            l_ct = step(lambda i=i, n=n: v.reduce_sum(s_cnt[0:1, i:i + 1], s_ptmp[0:1, 0:n], axis=AXL.X), after=(l_gt,))
            prev = step(lambda i=i: v.tensor_scalar(s_cnt[0:1, i:i + 1], s_cnt[0:1, i:i + 1], 1.0, None, ALU.max), after=(l_ct,))
        em.alias(SV, ("cnt",))
        em.wait(SA, ("cntln",))
        l1 = step(lambda: v.tensor_tensor(s_ptmp[0:1, 0:5], s_cnl[0:1, 0:5], s_lx[0:1, 0:5], ALU.mult))
        l2 = step(lambda: v.reduce_sum(s_dsc[0:1, 16:17], s_ptmp[0:1, 0:5], axis=AXL.X), after=(l1,))
        l3 = step(lambda: v.reduce_sum(s_dsc[0:1, 17:18], s_cnl[0:1, 0:5], axis=AXL.X))
        l4 = step(lambda: v.tensor_scalar(s_dsc[0:1, 18:19], s_dsc[0:1, 16:17], 5.0, None, ALU.mult), after=(l2,))
        l5 = step(lambda: v.tensor_scalar(s_dsc[0:1, 19:20], s_dsc[0:1, 17:18], SLX, None, ALU.mult), after=(l3,))
        l6 = step(lambda: v.tensor_tensor(s_dsc[0:1, 20:21], s_dsc[0:1, 18:19], s_dsc[0:1, 19:20], ALU.subtract), after=(l5,))
        l7 = step(lambda: v.tensor_scalar(s_dsc[0:1, 21:22], s_dsc[0:1, 20:21], -1.0 / DEN, None, ALU.mult), after=(l6,))
        l8 = step(lambda: v.tensor_scalar(s_dsc[0:1, 21:22], s_dsc[0:1, 21:22], 1.0, None, ALU.max), after=(l7,))
        l9 = step(lambda: v.tensor_scalar(s_dsc[0:1, 21:22], s_dsc[0:1, 21:22], 2.0, None, ALU.min), after=(l8,))
        l10 = step(lambda: v.tensor_tensor(s_dsc[0:1, 22:23], s_dsc[0:1, 21:22], s_dsc[0:1, 21:22], ALU.mult), after=(l9,))
        l11 = step(lambda: v.tensor_scalar(s_dsrow[0:1, 0:1], s_dsc[0:1, 22:23], 0.2, None, ALU.mult), after=(l10,))
        l12 = step(lambda: v.tensor_tensor(s_dsrow[0:1, 0:1], s_dsrow[0:1, 0:1], s_dsc[0:1, 21:22], ALU.add), after=(l11,))
        step(lambda: v.tensor_scalar(s_dsrow[0:1, 1:2], s_dsc[0:1, 21:22], 0.5, 1.0, ALU.mult, ALU.add))
        step(lambda: v.tensor_scalar(s_dsrow[0:1, 2:3], s_dsc[0:1, 21:22], 0.1, None, ALU.mult))
        step(lambda: v.tensor_scalar(s_dsrow[0:1, 3:4], s_dsc[0:1, 21:22], 0.0, None, ALU.mult), ("dsrow",))
        em.wait(SA, ("bcDev",))
        l_ang = step(lambda: v.tensor_scalar(s_ang[:], s_ka[:], s_bcD[:, 1:2], s_bcD[:, 0:1], ALU.mult, ALU.add))
        l_ph1 = step(lambda: v.tensor_scalar(s_ph[:], s_ka2[:], s_bcD[:, 2:3], None, ALU.mult))
        l_ph2 = step(lambda: v.tensor_tensor(s_ph[:], s_ph[:], s_ka[:], ALU.subtract), after=(l_ph1,))
        step(lambda: v.tensor_scalar(s_ph[:], s_ph[:], s_bcD[:, 0:1], None, ALU.add), ("ph",), after=(l_ph2,))
        l_mgt = step(lambda: v.tensor_scalar(s_mgt[:], s_ang[:], PI, None, ALU.is_gt), after=(l_ang,))
        step(lambda: v.scalar_tensor_tensor(s_zz[:], s_mgt[:], -PI, s_ang[:], ALU.mult, ALU.add), ("zz",), after=(l_mgt,))
        l_sgn = step(lambda: v.tensor_scalar(s_sgn[:], s_mgt[:], -2.0, 1.0, ALU.mult, ALU.add))
        em.wait(SA, ("sinzz",))
        l_sing = step(lambda: v.tensor_tensor(s_sing[:], s_sa[:], s_sgn[:], ALU.mult), after=(l_sgn,))
        em.wait(SA, ("cosp",))
        l_rf = step(lambda: v.tensor_tensor(s_rf[:], s_sing[:], s_cosp[:], ALU.mult), after=(l_sing,))
        step(lambda: v.tensor_copy(s_rfb[:], s_rf[:]), ("rfb",), after=(l_rf,))
        em.wait(SA, ("sinp",))
        l_if = step(lambda: v.tensor_tensor(s_if[:], s_sing[:], s_sinp[:], ALU.mult))
        step(lambda: v.tensor_copy(s_ifb[:], s_if[:]), ("ifb",), after=(l_if,))

        for grp in range(NG):
            em.wait(SP, ("msum", grp))
            if grp > 0:
                em.wait(SA, ("silu", grp - 1, 2))
            l_mm = step(lambda: v.tensor_scalar(s_mrow[:], banks[6][0:1, 0:GRP], 1.0 / E, None, ALU.mult), ("mmean", grp))
            em.wait(SP, ("sqsum", grp, 7))
            l_vr = step(lambda: v.tensor_scalar(s_vrow[:], banks[7][0:1, 0:GRP], 1.0 / E, None, ALU.mult))
            l_pt = step(lambda: v.tensor_tensor(s_ptmp[0:1, 0:GRP], s_mrow[:], s_mrow[:], ALU.mult),
                        after=(l_mm,) if grp else (l_mm, l2))
            step(lambda: v.tensor_tensor(s_vrow[:], s_vrow[:], s_ptmp[0:1, 0:GRP], ALU.subtract), ("var", grp), after=(l_pt,))
            em.wait(SP, ("mb", grp))
            em.wait(SP, ("rb", grp))
            lt = {}
            for dc in range(8):
                lt[dc] = step(lambda grp=grp, dc=dc: v.tensor_tensor(
                    s_xtmp[dc % 2][:], chx(grp, dc), banks[6][:, 0:GRP], ALU.subtract),
                    after=(("xn", grp, dc - 2),) if dc >= 2 else ())
                if dc >= 1:
                    step(lambda dc=dc: v.tensor_tensor(
                        ch(s_xnT, dc - 1), s_xtmp[(dc - 1) % 2][:], banks[7][:, 0:GRP], ALU.mult),
                        ("xn", grp, dc - 1), after=(lt[dc - 1],))
            step(lambda: v.tensor_tensor(
                ch(s_xnT, 7), s_xtmp[1][:], banks[7][:, 0:GRP], ALU.mult),
                ("xn", grp, 7), after=(lt[7],))
            lr = {}
            for dc in range(8):
                em.wait(PR[dc % 2], ("xTr", grp, dc))
                lr[dc] = step(lambda dc=dc: v.tensor_tensor(
                    s_xtmp[dc % 2][:], s_sq[dc % 2][:], banks[6][:, 0:GRP], ALU.subtract),
                    after=(("xnr", grp, dc - 2),) if dc >= 2 else
                          (("xn", grp, 6 + dc),))
                if dc >= 1:
                    step(lambda dc=dc: v.tensor_tensor(
                        ch(s_xnTr, dc - 1), s_xtmp[(dc - 1) % 2][:], banks[7][:, 0:GRP], ALU.mult),
                        ("xnr", grp, dc - 1), after=(lr[dc - 1],))
            step(lambda: v.tensor_tensor(
                ch(s_xnTr, 7), s_xtmp[1][:], banks[7][:, 0:GRP], ALU.mult),
                ("xnr", grp, 7), after=(lr[7],))
            for tt in range(4):
                em.wait(SP, ("logits", grp, tt))
                step(lambda tt=tt: v.reduce_max(
                    s_nmax[:, tt:tt + 1], banks[5][0:128, tt * 128:tt * 128 + 8],
                    axis=AXL.X, negate=True), ("negmax", grp, tt))
            for tt in range(4):
                sl = slice(tt * 8, (tt + 1) * 8)
                em.wait(SA, ("exp", grp, tt))
                a1 = step(lambda sl=sl, tt=tt: v.reduce_sum(s_ssum[:, tt:tt + 1], s_esm[:, sl], axis=AXL.X))
                a2 = step(lambda tt=tt: v.reciprocal(s_srec[:, tt:tt + 1], s_ssum[:, tt:tt + 1]), after=(a1,))
                a3 = step(lambda sl=sl, tt=tt: v.tensor_scalar(s_probs[:, sl], s_esm[:, sl], s_srec[:, tt:tt + 1], None, ALU.mult), after=(a2,))
                a4 = step(lambda sl=sl, tt=tt: v.reduce_max(s_v1[:, tt:tt + 1], s_probs[:, sl], axis=AXL.X), after=(a3,))
                a5 = step(lambda sl=sl, tt=tt: v.tensor_scalar(s_c1[:, sl], s_probs[:, sl], s_v1[:, tt:tt + 1], None, ALU.is_lt), after=(a4,))
                a6 = step(lambda sl=sl: v.tensor_tensor(s_p2[:, sl], s_probs[:, sl], s_c1[:, sl], ALU.mult), after=(a5,))
                a7 = step(lambda sl=sl, tt=tt: v.reduce_max(s_v2[:, tt:tt + 1], s_p2[:, sl], axis=AXL.X), after=(a6,))
                a8 = step(lambda sl=sl, tt=tt: v.tensor_scalar(s_gmask[:, sl], s_probs[:, sl], s_v2[:, tt:tt + 1], None, ALU.is_ge), after=(a7,))
                a9 = step(lambda sl=sl: v.tensor_tensor(s_gates[:, sl], s_probs[:, sl], s_gmask[:, sl], ALU.mult), after=(a8,))
                step(lambda sl=sl: v.tensor_copy(s_gatesb[:, sl], s_gates[:, sl]), ("gatesb", grp, tt), after=(a9,))
            for e in range(NE):
                if e > 0:
                    em.wait(SA, ("gbev", grp, e - 1))
                    for f in range(32):
                        em.wait(SA, ("gelu", grp, e, f))
                        step(lambda e=e, f=f: v.tensor_tensor(
                            s_gh[:, f * GRP:(f + 1) * GRP], s_gtmp[f % 2][:],
                            ch(s_gb, e - 1), ALU.mult), ("gh", grp, e, f))
                em.wait(SP, ("mm2end", grp, e))
                if e == 0:
                    if grp > 0:
                        em.wait(SA, ("silu", grp - 1, 7))
                else:
                    em.wait(SV, ("yev", grp, e - 1, 7))
                for dp in range(8):
                    if e == 0:
                        step(lambda dp=dp: v.tensor_copy(ch(s_acc, dp), banks[dp][:]),
                             ("yev", grp, e, dp))
                    else:
                        step(lambda dp=dp: v.tensor_tensor(
                            ch(s_acc, dp), ch(s_acc, dp), banks[dp][:], ALU.add),
                            ("yev", grp, e, dp))
            em.wait(SP, ("lat", grp))
            l_bf = step(lambda: v.tensor_scalar(s_bitsf[:], banks[0][0:24, 0:GRP], s_nbel[:], None, ALU.is_gt), after=(l_nbel,))
            step(lambda: v.tensor_copy(s_bits[:], s_bitsf[:]), ("bits", grp), after=(l_bf,))
            em.wait(SP, ("syn", grp))
            l_sf = step(lambda: v.tensor_scalar(s_synpf[0:12, :], banks[1][0:12, 0:GRP], 0.5, None, ALU.is_gt), after=(l_syn0,))
            step(lambda: v.tensor_copy(s_synp[:], s_synpf[0:12, :]), ("synp", grp), after=(l_sf,))
            em.wait(SP, ("wsum", grp))
            l_f1 = step(lambda: v.tensor_scalar(s_ptmp[0:1, 0:GRP], banks[2][0:1, 0:GRP], 0.5, None, ALU.is_gt))
            l_f2 = step(lambda: v.tensor_scalar(s_ptmp[0:1, GRP:2 * GRP], banks[2][0:1, 0:GRP], 3.5, None, ALU.is_lt))
            step(lambda: v.tensor_tensor(s_fxb[:], s_ptmp[0:1, 0:GRP], s_ptmp[0:1, GRP:2 * GRP], ALU.mult), ("fix", grp), after=(l_f2,))
            em.wait(SP, ("fixbc", grp))
            l_rT = step(lambda: v.tensor_tensor(s_rT[:], s_synpf[:], banks[3][0:24, 0:GRP], ALU.mult))
            l_uT = step(lambda: v.tensor_tensor(s_uT[:], s_rT[:], s_bitsf[:], ALU.mult), after=(l_rT,))
            l_rT2 = step(lambda: v.tensor_tensor(s_rT[:], s_bitsf[:], s_rT[:], ALU.add), after=(l_uT,))
            step(lambda: v.scalar_tensor_tensor(s_corr[:], s_uT[:], -2.0, s_rT[:], ALU.mult, ALU.add), ("corr", grp), after=(l_rT2,))
            for dp in range(8):
                em.wait(SP, ("smm", grp, dp))
                l_t1 = step(lambda dp=dp: v.tensor_scalar(
                    s_t1[dp % 2][:], banks[4 + dp % 2][:], s_ble[:, dp:dp + 1],
                    None, ALU.add), ("t1", grp, dp),
                    after=(("sev", grp, dp - 2),) if dp >= 2 else ())
                step(lambda dp=dp: v.tensor_tensor(
                    ch(s_sT, dp), s_t1[dp % 2][:], ch(s_x1, dp), ALU.add),
                    ("sev", grp, dp), after=(l_t1,))
            for dp in range(8):
                em.wait(SA, ("silu", grp, dp))
                l_pr = step(lambda dp=dp: v.tensor_tensor(
                    s_pch[dp % 2][:], s_gch[dp % 2][:], ch(s_sT, dp), ALU.mult),
                    ("prod", grp, dp),
                    after=(("sev", grp, dp), ("out", grp, dp - 2))
                    if dp >= 2 else (("sev", grp, dp),))
                if grp > 0 or dp >= 2:
                    pv = grp * 8 + dp - 2
                    em.wait(SO[dp % 2], ("st", pv // 8, pv % 8))
                step(lambda grp=grp, dp=dp: v.tensor_tensor(
                    s_och[dp % 2][:], s_pch[dp % 2][:], chx(grp, dp), ALU.add),
                    ("out", grp, dp), after=(l_pr,))

    labels = {}
    streams = [
        ("sync", stream_sync), ("gpsimd", stream_gpsimd),
        ("tensor", stream_tensor), ("scalar", stream_scalar),
        ("vector", stream_vector),
    ]
    for _, fn in streams:
        fn(Em("count", labels))

    engmap = {"sync": blk.sync, "gpsimd": blk.gpsimd, "tensor": blk.tensor,
              "scalar": blk.scalar, "vector": blk.vector}
    for name, fn in streams:
        engmap[name](lambda eng, fn=fn: fn(Em("emit", labels, eng)))

    es.close()
    return nc


def _host_inputs(inputs):
    f32 = np.float32
    x = np.asarray(inputs["L0_attn_x"], dtype=f32)
    W1 = np.asarray(inputs["L0_ffn_W1"], dtype=f32)
    W2 = np.asarray(inputs["L0_ffn_W2"], dtype=f32)
    Ws1 = np.asarray(inputs["L0_ffn_Ws1"], dtype=f32)
    Ws2 = np.asarray(inputs["L0_ffn_Ws2"], dtype=f32)
    wr = np.asarray(inputs["L0_ffn_w_router"], dtype=f32)
    Wel = np.asarray(inputs["L0_attn_W_el"], dtype=f32)
    Wle = np.asarray(inputs["L0_attn_W_le"], dtype=f32)
    b1 = np.asarray(inputs["L0_ffn_b1"], dtype=f32)
    bs1 = np.asarray(inputs["L0_ffn_bs1"], dtype=f32)
    b2 = np.asarray(inputs["L0_ffn_b2"], dtype=f32)
    bs2 = np.asarray(inputs["L0_ffn_bs2"], dtype=f32)
    ble = np.asarray(inputs["L0_attn_b_le"], dtype=f32)
    bel = np.asarray(inputs["L0_attn_b_el"], dtype=f32)

    def c(a):
        return np.ascontiguousarray(a)

    W1TT = c(W1.transpose(0, 2, 1).reshape(8, 8, 128, 8, 512)
             .transpose(0, 3, 2, 1, 4).reshape(8, 8, 128, 4096))
    Ws1TT = c(Ws1.T.reshape(8, 128, 8, 512).transpose(2, 1, 0, 3)
              .reshape(8, 128, 4096))
    W2TT = c(W2.transpose(0, 2, 1).reshape(8, 8, 4, 128, 1024)
             .transpose(0, 1, 3, 2, 4).reshape(8, 8, 128, 4096))
    Ws2TT = c(Ws2.T.reshape(8, 4, 128, 1024).transpose(0, 2, 1, 3)
              .reshape(8, 128, 4096))
    wrT3 = c(wr.T.reshape(8, 128, 8))
    WelT3 = c(Wel.reshape(8, 128, 24))
    b1r = c(b1.reshape(8, 32, 128).transpose(0, 2, 1))
    bs1r = c(bs1.reshape(32, 128).T)
    bs2r = c(bs2.reshape(8, 128).T)
    bler = c(ble.reshape(8, 128).T)
    belr = c(bel.reshape(24, 1))

    _A = np.array([
        [1, 1, 1, 0, 1, 1, 1, 0, 0, 0, 1, 1], [1, 1, 0, 1, 1, 1, 0, 0, 0, 1, 1, 0],
        [1, 0, 1, 1, 1, 0, 0, 0, 1, 1, 0, 1], [0, 1, 1, 1, 0, 0, 0, 1, 1, 0, 1, 1],
        [0, 0, 0, 0, 1, 1, 1, 1, 1, 1, 1, 1], [1, 1, 1, 1, 0, 0, 0, 0, 0, 0, 0, 0],
        [0, 0, 0, 0, 1, 1, 1, 1, 0, 0, 0, 0], [1, 1, 0, 0, 1, 1, 0, 0, 1, 1, 0, 0],
        [1, 0, 1, 0, 1, 0, 1, 0, 1, 0, 1, 0], [0, 1, 0, 1, 0, 1, 0, 1, 0, 1, 0, 1],
        [1, 0, 0, 1, 1, 0, 0, 1, 1, 0, 0, 1], [0, 1, 1, 0, 0, 1, 1, 0, 0, 1, 1, 0]],
        dtype=f32)
    H = np.concatenate([_A.T, np.eye(12, dtype=f32)], axis=1)
    HT = c(H.T).astype(BF)

    ident = np.eye(128, dtype=f32).astype(BF)
    sel8 = np.zeros((8, 8 * 128), dtype=f32)
    for e in range(8):
        sel8[e, e * 128:(e + 1) * 128] = 1.0
    sel8 = sel8.astype(BF)

    k = np.fft.fftfreq(E).astype(np.float64)
    kav = (np.abs(k) + 1e-8).astype(f32)
    ka = c(kav.reshape(8, 128).T)
    ka2 = c((kav.astype(np.float64) ** 2).astype(f32).reshape(8, 128).T)
    lx = np.zeros((1, 8), dtype=f32)
    lx[0, :5] = np.log([2.0, 4.0, 8.0, 16.0, 32.0])
    lx[0, 5] = 1e-5

    kk = np.arange(E, dtype=np.float64)
    ang = 2.0 * np.pi * np.outer(kk, kk) / E
    Ck = np.cos(ang).astype(f32).reshape(8, 128, E).astype(BF)
    SkN = (-np.sin(ang)).astype(f32).reshape(8, 128, E).astype(BF)

    shared = {
        "W1TT": W1TT, "W2TT": W2TT, "Ws1TT": Ws1TT, "Ws2TT": Ws2TT,
        "wrT3": wrT3, "WelT3": WelT3, "Wle": c(Wle),
        "b1r": b1r, "bs1r": bs1r, "b2": c(b2), "bs2r": bs2r,
        "bler": bler, "bel": belr, "HT": HT, "ident": ident, "sel8": sel8,
        "onesf": np.stack([np.ones(128, dtype=f32), np.full(128, np.pi / 2, dtype=f32)], axis=1),
        "onesrf": np.ones((1, 128), dtype=f32),
        "onesrb": np.ones((1, 128), dtype=f32).astype(BF),
        "ones12b": np.ones((12, 1), dtype=f32).astype(BF),
        "ka": ka, "ka2": ka2, "lx": lx, "Ck": c(Ck), "SkN": c(SkN),
        "xfirst": c(x[0, 0].reshape(1, E)),
    }
    in_maps = []
    rows = B // NCORE
    for core in range(NCORE):
        xs = x[core * rows:(core + 1) * rows].reshape(TC, E)
        m = dict(shared)
        m["xT"] = c(xs.T)
        m["xTrev"] = c(xs.T[::-1])
        in_maps.append(m)
    return in_maps


def kernel(**inputs):
    in_maps = _host_inputs(inputs)
    nc = build_program()
    res = run_bass_kernel_spmd(nc, in_maps, core_ids=list(range(NCORE)))
    outs = []
    for core in range(NCORE):
        oT = res.results[core]["outT"]
        outs.append(np.ascontiguousarray(oT.T))
    return np.concatenate(outs, axis=0).reshape(B, S, E)


# version 10
# speedup vs baseline: 1.1503x; 1.0012x over previous
import math
from contextlib import ExitStack

import numpy as np
import ml_dtypes

import concourse.bass as bass
import concourse.mybir as mybir
from concourse.bass_utils import run_bass_kernel_spmd

F32 = mybir.dt.float32
BF16 = mybir.dt.bfloat16
AF = mybir.ActivationFunctionType
ALU = mybir.AluOpType
AXL = mybir.AxisListType
BF = ml_dtypes.bfloat16

B, S, E = 16, 2048, 1024
NCORE = 8
TC = B * S // NCORE
GRP = 512
NG = TC // GRP
NEXP = 8
NE = NEXP + 1
SCALE = 1.0 / math.sqrt(1.0 + 2.0 / 8.0)
EPS = 1e-5
PI = math.pi

_lx = np.log(np.array([2.0, 4.0, 8.0, 16.0, 32.0], dtype=np.float64))
SLX = float(_lx.sum())
DEN = float(5.0 * (_lx * _lx).sum() - SLX * SLX)


class Sem:
    def __init__(self, name):
        self.name = name
        self.h = None


class Em:

    def __init__(self, mode, labels, eng=None):
        self.mode = mode
        self.labels = labels
        self.counters = {}
        self.eng = eng
        self._last = None

    def do(self, fn):
        if self.mode == "emit":
            self._last = fn()

    def inc(self, sem, label=None, amt=1):
        v = self.counters.get(sem.name, 0) + amt
        self.counters[sem.name] = v
        if label is not None:
            key = (sem.name, label)
            if self.mode == "count":
                assert key not in self.labels, f"dup label {key}"
                self.labels[key] = v
        if self.mode == "emit":
            self._last.then_inc(sem.h, amt)

    def alias(self, sem, label):
        v = self.counters.get(sem.name, 0)
        if self.mode == "count":
            key = (sem.name, label)
            assert key not in self.labels, f"dup label {key}"
            self.labels[key] = v

    def wait(self, sem, label, delta=0):
        if self.mode == "emit":
            self.eng.wait_ge(sem.h, self.labels[(sem.name, label)] + delta)

    def wait_val(self, sem, v):
        if self.mode == "emit":
            self.eng.wait_ge(sem.h, v)


def build_program():
    nc = bass.Bass("TRN2", target_bir_lowering=False)

    d_xT = nc.dram_tensor("xT", [E, TC], F32, kind="ExternalInput").ap()
    d_xTrev = nc.dram_tensor("xTrev", [E, TC], F32, kind="ExternalInput").ap()
    d_xfirst = nc.dram_tensor("xfirst", [1, E], F32, kind="ExternalInput").ap()
    d_w1 = nc.dram_tensor("W1TT", [NEXP, 8, 128, 4096], F32, kind="ExternalInput").ap()
    d_w2 = nc.dram_tensor("W2TT", [NEXP, 8, 128, 4096], F32, kind="ExternalInput").ap()
    d_ws1 = nc.dram_tensor("Ws1TT", [8, 128, 4096], F32, kind="ExternalInput").ap()
    d_ws2 = nc.dram_tensor("Ws2TT", [8, 128, 4096], F32, kind="ExternalInput").ap()
    d_wr = nc.dram_tensor("wrT3", [8, 128, 8], F32, kind="ExternalInput").ap()
    d_wel = nc.dram_tensor("WelT3", [8, 128, 24], F32, kind="ExternalInput").ap()
    d_wle = nc.dram_tensor("Wle", [24, E], F32, kind="ExternalInput").ap()
    d_b1 = nc.dram_tensor("b1r", [NEXP, 128, 32], F32, kind="ExternalInput").ap()
    d_bs1 = nc.dram_tensor("bs1r", [128, 32], F32, kind="ExternalInput").ap()
    d_b2 = nc.dram_tensor("b2", [8, E], F32, kind="ExternalInput").ap()
    d_bs2 = nc.dram_tensor("bs2r", [128, 8], F32, kind="ExternalInput").ap()
    d_ble = nc.dram_tensor("bler", [128, 8], F32, kind="ExternalInput").ap()
    d_bel = nc.dram_tensor("bel", [24, 1], F32, kind="ExternalInput").ap()
    d_ht = nc.dram_tensor("HT", [24, 12], BF16, kind="ExternalInput").ap()
    d_ident = nc.dram_tensor("ident", [128, 128], BF16, kind="ExternalInput").ap()
    d_sel = nc.dram_tensor("sel8", [8, 8 * 128], BF16, kind="ExternalInput").ap()
    d_ones = nc.dram_tensor("onesf", [128, 2], F32, kind="ExternalInput").ap()
    d_onesr = nc.dram_tensor("onesrf", [1, 128], F32, kind="ExternalInput").ap()
    d_onesb = nc.dram_tensor("onesrb", [1, 128], BF16, kind="ExternalInput").ap()
    d_ones12 = nc.dram_tensor("ones12b", [12, 1], BF16, kind="ExternalInput").ap()
    d_ka = nc.dram_tensor("ka", [128, 8], F32, kind="ExternalInput").ap()
    d_ka2 = nc.dram_tensor("ka2", [128, 8], F32, kind="ExternalInput").ap()
    d_lx = nc.dram_tensor("lx", [1, 8], F32, kind="ExternalInput").ap()
    d_ck = nc.dram_tensor("Ck", [8, 128, 1024], BF16, kind="ExternalInput").ap()
    d_skn = nc.dram_tensor("SkN", [8, 128, 1024], BF16, kind="ExternalInput").ap()
    d_out = nc.dram_tensor("outT", [E, TC], F32, kind="ExternalOutput").ap()

    d_w1bf = nc.dram_tensor("w1bf", [NEXP, 8, 128, 4096], BF16, kind="Internal").ap()
    d_w2bf = nc.dram_tensor("w2bf", [NEXP, 8, 128, 4096], BF16, kind="Internal").ap()
    d_ws1bf = nc.dram_tensor("ws1bf", [8, 128, 4096], BF16, kind="Internal").ap()
    d_ws2bf = nc.dram_tensor("ws2bf", [8, 128, 4096], BF16, kind="Internal").ap()
    d_hh = nc.dram_tensor("hh", [2048], BF16, kind="Internal").ap()

    es = ExitStack()
    E_ = es.enter_context

    s_xT = [E_(nc.sbuf_tensor(f"s_xT{i}", [128, 8 * GRP], F32)) for i in range(2)]
    s_xnT = E_(nc.sbuf_tensor("s_xnT", [128, 8 * GRP], BF16))
    s_xnTr = E_(nc.sbuf_tensor("s_xnTr", [128, 8 * GRP], BF16))
    s_acc = E_(nc.sbuf_tensor("s_acc", [128, 8 * GRP], F32))
    s_gh = E_(nc.sbuf_tensor("s_gh", [128, 32 * GRP], BF16))
    s_gb = E_(nc.sbuf_tensor("s_gb", [128, 8 * GRP], BF16))
    s_w1b = [E_(nc.sbuf_tensor(f"s_w1b{i}", [128, 8 * 512], BF16)) for i in range(2)]
    s_w2b = [E_(nc.sbuf_tensor(f"s_w2b{i}", [128, 4 * 1024], BF16)) for i in range(2)]
    s_msb = [E_(nc.sbuf_tensor(f"s_msb{i}", [128, 1024], BF16)) for i in range(2)]
    s_x1 = E_(nc.sbuf_tensor("s_x1", [128, 8 * GRP], BF16))
    s_sT = E_(nc.sbuf_tensor("s_sT", [128, 8 * GRP], BF16))
    s_gtmp = [E_(nc.sbuf_tensor(f"s_gtmp{i}", [128, GRP], BF16)) for i in range(2)]
    s_sq = [E_(nc.sbuf_tensor(f"s_sq{i}", [128, GRP], F32)) for i in range(2)]
    s_xtmp = [E_(nc.sbuf_tensor(f"s_xtmp{i}", [128, GRP], F32)) for i in range(2)]
    s_gch = [E_(nc.sbuf_tensor(f"s_gch{i}", [128, GRP], BF16)) for i in range(2)]
    s_pch = [E_(nc.sbuf_tensor(f"s_pch{i}", [128, GRP], F32)) for i in range(2)]
    s_och = [E_(nc.sbuf_tensor(f"s_och{i}", [128, GRP], F32)) for i in range(2)]
    s_t1 = [E_(nc.sbuf_tensor(f"s_t1_{i}", [128, GRP], BF16)) for i in range(2)]

    s_wr = E_(nc.sbuf_tensor("s_wr", [128, 64], BF16))
    s_wel = E_(nc.sbuf_tensor("s_wel", [128, 8 * 24], BF16))
    s_wle = E_(nc.sbuf_tensor("s_wle", [24, E], BF16))
    s_b1 = E_(nc.sbuf_tensor("s_b1", [128, NEXP * 32], F32))
    s_bs1 = E_(nc.sbuf_tensor("s_bs1", [128, 32], F32))
    s_b2 = E_(nc.sbuf_tensor("s_b2", [8, E], BF16))
    s_bs2 = E_(nc.sbuf_tensor("s_bs2", [128, 8], F32))
    s_bs2s = E_(nc.sbuf_tensor("s_bs2s", [128, 8], F32))
    s_ble = E_(nc.sbuf_tensor("s_ble", [128, 8], F32))
    s_bel = E_(nc.sbuf_tensor("s_bel", [24, 1], F32))
    s_nbel = E_(nc.sbuf_tensor("s_nbel", [24, 1], F32))
    s_ht = E_(nc.sbuf_tensor("s_ht", [24, 12], BF16))
    s_id = E_(nc.sbuf_tensor("s_id", [128, 128], BF16))
    s_sel = E_(nc.sbuf_tensor("s_sel", [8, 8 * 128], BF16))
    s_onesc = E_(nc.sbuf_tensor("s_onesc", [128, 2], F32))
    s_onesr = E_(nc.sbuf_tensor("s_onesr", [1, 128], F32))
    s_onesb = E_(nc.sbuf_tensor("s_onesb", [1, 128], BF16))
    s_ones12 = E_(nc.sbuf_tensor("s_ones12", [12, 1], BF16))
    s_ka = E_(nc.sbuf_tensor("s_ka", [128, 8], F32))
    s_ka2 = E_(nc.sbuf_tensor("s_ka2", [128, 8], F32))
    s_lx = E_(nc.sbuf_tensor("s_lx", [1, 8], F32))

    s_nmax = E_(nc.sbuf_tensor("s_nmax", [128, 4], F32))
    s_esm = E_(nc.sbuf_tensor("s_esm", [128, 32], F32))
    s_ssum = E_(nc.sbuf_tensor("s_ssum", [128, 4], F32))
    s_srec = E_(nc.sbuf_tensor("s_srec", [128, 4], F32))
    s_probs = E_(nc.sbuf_tensor("s_probs", [128, 32], F32))
    s_v1 = E_(nc.sbuf_tensor("s_v1", [128, 4], F32))
    s_c1 = E_(nc.sbuf_tensor("s_c1", [128, 32], F32))
    s_p2 = E_(nc.sbuf_tensor("s_p2", [128, 32], F32))
    s_v2 = E_(nc.sbuf_tensor("s_v2", [128, 4], F32))
    s_gmask = E_(nc.sbuf_tensor("s_gmask", [128, 32], F32))
    s_gates = E_(nc.sbuf_tensor("s_gates", [128, 32], F32))
    s_gatesb = E_(nc.sbuf_tensor("s_gatesb", [128, 32], BF16))
    s_gT = E_(nc.sbuf_tensor("s_gT", [8, GRP], BF16))
    s_bitsf = E_(nc.sbuf_tensor("s_bitsf", [24, GRP], F32))
    s_bits = E_(nc.sbuf_tensor("s_bits", [24, GRP], BF16))
    s_synpf = E_(nc.sbuf_tensor("s_synpf", [24, GRP], F32))
    s_synp = E_(nc.sbuf_tensor("s_synp", [12, GRP], BF16))
    s_fxb = E_(nc.sbuf_tensor("s_fxb", [1, GRP], BF16))
    s_rT = E_(nc.sbuf_tensor("s_rT", [24, GRP], F32))
    s_uT = E_(nc.sbuf_tensor("s_uT", [24, GRP], F32))
    s_corr = E_(nc.sbuf_tensor("s_corr", [24, GRP], BF16))

    s_cnt = E_(nc.sbuf_tensor("s_cnt", [1, 8], F32))
    s_cnl = E_(nc.sbuf_tensor("s_cnl", [1, 8], F32))
    s_ptmp = E_(nc.sbuf_tensor("s_ptmp", [1, E], F32))
    s_dsrow = E_(nc.sbuf_tensor("s_dsrow", [1, 4], F32))
    s_bcD = E_(nc.sbuf_tensor("s_bcD", [128, 4], F32))
    s_ang = E_(nc.sbuf_tensor("s_ang", [128, 8], F32))
    s_ph = E_(nc.sbuf_tensor("s_ph", [128, 8], F32))
    s_mgt = E_(nc.sbuf_tensor("s_mgt", [128, 8], F32))
    s_zz = E_(nc.sbuf_tensor("s_zz", [128, 8], F32))
    s_sgn = E_(nc.sbuf_tensor("s_sgn", [128, 8], F32))
    s_sa = E_(nc.sbuf_tensor("s_sa", [128, 8], F32))
    s_sing = E_(nc.sbuf_tensor("s_sing", [128, 8], F32))
    s_cosp = E_(nc.sbuf_tensor("s_cosp", [128, 8], F32))
    s_sinp = E_(nc.sbuf_tensor("s_sinp", [128, 8], F32))
    s_rf = E_(nc.sbuf_tensor("s_rf", [128, 8], F32))
    s_if = E_(nc.sbuf_tensor("s_if", [128, 8], F32))
    s_rfb = E_(nc.sbuf_tensor("s_rfb", [128, 8], BF16))
    s_ifb = E_(nc.sbuf_tensor("s_ifb", [128, 8], BF16))
    s_hreb = E_(nc.sbuf_tensor("s_hreb", [1, E], BF16))

    banks = [E_(nc.psum_tensor(f"bank{i}", [128, 512], F32)) for i in range(8)]

    s_xn00 = s_acc[0:1, 0:1024]
    s_dsc = s_acc[0:1, 1536:1568]
    s_mrow = s_acc[0:1, 0:512]
    s_vrow = s_acc[0:1, 512:1024]
    s_rrow = s_acc[0:1, 1024:1536]

    SP = Sem("SP"); SA = Sem("SA"); SV = Sem("SV")
    PC = Sem("PC"); PK = Sem("PK"); PH = Sem("PH")
    PT = [Sem("PTa"), Sem("PTb")]
    PR = [Sem("PRa"), Sem("PRb")]
    PM = [Sem("PMa"), Sem("PMb")]
    SO = [Sem("SOa"), Sem("SOb")]
    PWC = Sem("PWC")
    PW1 = [Sem("PW1a"), Sem("PW1b")]
    PW2 = [Sem("PW2a"), Sem("PW2b")]
    PS2 = [Sem("PS2a"), Sem("PS2b")]
    PB1 = [Sem("PB1a"), Sem("PB1b")]
    PB2 = [Sem("PB2a"), Sem("PB2b")]
    for s in (SP, SA, SV, PC, PK, PH, *PT, *PR, *PM, *SO, PWC, *PW1, *PW2,
              *PS2, *PB1, *PB2):
        s.h = es.enter_context(nc.semaphore(s.name))

    blk = es.enter_context(nc.Block())

    def ch(t, dc, w=GRP):
        return t[:, dc * w:(dc + 1) * w]

    def chx(grp, dc):
        return s_xT[grp % 2][:, dc * GRP:(dc + 1) * GRP]

    NOUT_DMAS = NG * 8

    def stream_sync(em):
        g = em.eng
        em.do(lambda: g.dma_start(s_onesc[:], d_ones)); em.inc(PC, None, 16)
        em.do(lambda: g.dma_start(s_onesr[:], d_onesr)); em.inc(PC, None, 16)
        em.do(lambda: g.dma_start(s_onesb[:], d_onesb)); em.inc(PC, None, 16)
        em.do(lambda: g.dma_start(s_ones12[:], d_ones12)); em.inc(PC, None, 16)
        em.do(lambda: g.dma_start(s_ka[:], d_ka)); em.inc(PC, None, 16)
        em.do(lambda: g.dma_start(s_ka2[:], d_ka2)); em.inc(PC, None, 16)
        em.do(lambda: g.dma_start(s_lx[:], d_lx)); em.inc(PC, None, 16)
        em.do(lambda: g.dma_start(s_ht[:], d_ht)); em.inc(PC, None, 16)
        em.do(lambda: g.dma_start(s_id[:], d_ident)); em.inc(PC, None, 16)
        em.do(lambda: g.dma_start(s_sel[:], d_sel)); em.inc(PC, None, 16)
        em.do(lambda: g.dma_start(
            s_b1[:].rearrange("p (e j) -> p e j", e=NEXP),
            d_b1.rearrange("e p j -> p e j")))
        em.inc(PC, None, 16)
        em.do(lambda: g.dma_start(s_bs1[:], d_bs1)); em.inc(PC, None, 16)
        em.do(lambda: g.dma_start(s_bs2[:], d_bs2)); em.inc(PC, None, 16)
        em.do(lambda: g.dma_start(s_ble[:], d_ble)); em.inc(PC, None, 16)
        em.do(lambda: g.dma_start(s_bel[:], d_bel)); em.inc(PC, None, 16)
        em.do(lambda: g.dma_start(s_ptmp[:], d_xfirst)); em.inc(PC, ("consts",), 16)
        for c in range(8):
            if c > 0:
                em.wait(SP, ("hre", c - 1))
            em.do(lambda c=c: g.dma_start(s_msb[0][:], d_ck[c]))
            em.inc(PK, None, 16)
            em.do(lambda c=c: g.dma_start(s_msb[1][:], d_skn[c]))
            em.inc(PK, ("ckskn", c), 16)
        em.wait(SA, ("hrev", 1))
        em.do(lambda: g.dma_start(bass.AP(d_hh.tensor, 0, [[1, 1], [1, 1024]]), s_hreb[0:1, :]))
        em.inc(PH, None, 16)
        em.do(lambda: g.dma_start(bass.AP(d_hh.tensor, 1024, [[1, 1], [1, 1024]]), s_hreb[0:1, :]))
        em.inc(PH, ("hh",), 16)

        for grp in range(2):
            for dc in range(8):
                em.do(lambda grp=grp, dc=dc: g.dma_start(
                    s_xT[grp % 2][:, dc * GRP:(dc + 1) * GRP],
                    d_xT[dc * 128:(dc + 1) * 128, grp * GRP:(grp + 1) * GRP]))
                em.inc(PT[grp % 2], ("xTg", grp) if dc == 7 else None, 16)

        for grp in range(NG):
            for dc in range(8):
                if dc < 2:
                    em.wait(SP, ("sqsum", grp, 7))
                else:
                    em.wait(SV, ("xnr", grp, dc - 2))
                em.do(lambda grp=grp, dc=dc: g.dma_start(
                    s_sq[dc % 2][:],
                    d_xTrev[dc * 128:(dc + 1) * 128, grp * GRP:(grp + 1) * GRP]))
                em.inc(PR[dc % 2], ("xTr", grp, dc), 16)
            if grp == 0:
                for e in range(NE):
                    for fb in range(8):
                        k = e * 8 + fb
                        em.wait(PW1[k % 2], ("w1", 0, e, fb))
                        dst = d_ws1bf[fb] if e == 0 else d_w1bf[e - 1][fb]
                        em.do(lambda dst=dst, b=(k + 1) % 2: g.dma_start(
                            dst, s_w1b[b][:]))
                        em.inc(PB1[k % 2], ("wb1", k), 16)
                    for pk in range(8):
                        k = e * 8 + pk
                        em.wait(PW2[k % 2], ("w2", 0, e, pk))
                        dst = d_ws2bf[pk] if e == 0 else d_w2bf[e - 1][pk]
                        em.do(lambda dst=dst, b=(k + 1) % 2: g.dma_start(
                            dst, s_w2b[b][:]))
                        em.inc(PB2[k % 2], ("wb2", k), 16)
            for m in range(8):
                if grp == 0 and m < 2:
                    em.wait(PH, ("hh",))
                    em.wait(SP, ("hre", 7))
                else:
                    pm = (grp * 8 + m) - 2
                    em.wait(SP, ("convm", pm // 8, pm % 8))
                em.do(lambda m=m: g.dma_start(
                    s_msb[m % 2][:],
                    bass.AP(d_hh.tensor, m * 128 + 1, [[1, 128], [1, 1024]])))
                em.inc(PM[m % 2], ("ms", grp, m), 16)
            for dp in range(8):
                em.wait(SV, ("out", grp, dp))
                em.do(lambda grp=grp, dp=dp: g.dma_start(
                    d_out[dp * 128:(dp + 1) * 128, grp * GRP:(grp + 1) * GRP],
                    s_och[dp % 2][:]))
                em.inc(SO[dp % 2], ("st", grp, dp), 16)
            if grp + 2 < NG:
                for dc in range(8):
                    em.do(lambda grp=grp, dc=dc: g.dma_start(
                        s_xT[grp % 2][:, dc * GRP:(dc + 1) * GRP],
                        d_xT[dc * 128:(dc + 1) * 128,
                             (grp + 2) * GRP:(grp + 3) * GRP]))
                    em.inc(PT[grp % 2], ("xTg", grp + 2) if dc == 7 else None, 16)
        em.wait_val(SO[0], 16 * (NOUT_DMAS // 2))
        em.wait_val(SO[1], 16 * (NOUT_DMAS // 2))

    def stream_gpsimd(em):
        g = em.eng
        em.do(lambda: g.dma_start(
            s_wr[:].rearrange("p (c j) -> p c j", c=8),
            d_wr.rearrange("c p j -> p c j")))
        em.inc(PWC, None, 16)
        em.do(lambda: g.dma_start(
            s_wel[:].rearrange("p (c j) -> p c j", c=8),
            d_wel.rearrange("c p j -> p c j")))
        em.inc(PWC, None, 16)
        em.do(lambda: g.dma_start(s_wle[:], d_wle)); em.inc(PWC, None, 16)
        em.do(lambda: g.dma_start(s_b2[:], d_b2)); em.inc(PWC, ("smallcast",), 16)

        wslab = 0
        wpack = 0
        for grp in range(NG):
            for e in range(NE):
                for fb in range(8):
                    wslab += 1
                    if wslab > 2:
                        i2 = wslab - 2
                        g2, r2 = divmod(i2 - 1, NE * 8)
                        e2, fb2 = divmod(r2, 8)
                        em.wait(SP, ("mm1", g2, e2, fb2 * 4 + 3))
                        if i2 <= NE * 8:
                            em.wait(PB1[(i2 - 1) % 2], ("wb1", i2 - 1))
                    if grp == 0:
                        w1src = d_ws1 if e == 0 else d_w1[e - 1]
                    else:
                        w1src = d_ws1bf if e == 0 else d_w1bf[e - 1]
                    em.do(lambda w1src=w1src, fb=fb, b=wslab % 2: g.dma_start(
                        s_w1b[b][:], w1src[fb]))
                    em.inc(PW1[(wslab - 1) % 2], ("w1", grp, e, fb), 16)
                if grp == 0:
                    for pk in range(8):
                        wpack += 1
                        if wpack > 2:
                            j2 = wpack - 2
                            e2, pk2 = divmod(j2 - 1, 8)
                            em.wait(SP, ("mm2p", 0, e2, pk2))
                            em.wait(PB2[(j2 - 1) % 2], ("wb2", j2 - 1))
                        w2src = d_ws2 if e == 0 else d_w2[e - 1]
                        em.do(lambda w2src=w2src, pk=pk, b=wpack % 2: g.dma_start(
                            s_w2b[b][:], w2src[pk]))
                        em.inc(PW2[(wpack - 1) % 2], ("w2", 0, e, pk), 16)

    def stream_tensor(em):
        t = em.eng
        bfree = {i: None for i in range(8)}

        def use_bank(i):
            if bfree[i] is not None:
                em.wait(*bfree[i])
                bfree[i] = None

        em.wait(SV, ("dsrow",))
        em.do(lambda: t.matmul(banks[2][0:128, 0:4], s_onesr[:], s_dsrow[:],
                               start=True, stop=True))
        em.inc(SP, ("bcD",), 1)
        em.wait(SV, ("ifb",))
        for c in range(8):
            em.wait(PK, ("ckskn", c))
            for half in range(2):
                em.do(lambda c=c, half=half: t.matmul(
                    banks[half][0:1, 0:512], s_rfb[:, c:c + 1],
                    s_msb[0][:, half * 512:(half + 1) * 512],
                    start=(c == 0), stop=False))
                em.do(lambda c=c, half=half: t.matmul(
                    banks[half][0:1, 0:512], s_ifb[:, c:c + 1],
                    s_msb[1][:, half * 512:(half + 1) * 512],
                    start=False, stop=(c == 7)))
            em.inc(SP, ("hre", c), 1)
        bfree[0] = (SA, ("hrev", 0))
        bfree[1] = (SA, ("hrev", 1))
        bfree[2] = (SA, ("bcDev",))
        em.wait(PWC, ("smallcast",))

        for grp in range(NG):
            use_bank(6)
            em.wait(PT[grp % 2], ("xTg", grp))
            for dc in range(8):
                em.do(lambda grp=grp, dc=dc: t.matmul(
                    banks[6][0:1, 0:GRP], s_onesc[:, 0:1], chx(grp, dc),
                    start=(dc == 0), stop=(dc == 7)))
            em.inc(SP, ("msum", grp), 1)
            use_bank(7)
            for dc in range(8):
                em.wait(SA, ("sq", grp, dc))
                em.do(lambda dc=dc: t.matmul(
                    banks[7][0:1, 0:GRP], s_onesc[:, 0:1], s_sq[dc % 2][:],
                    start=(dc == 0), stop=(dc == 7)))
                em.inc(SP, ("sqsum", grp, dc), 1)
            em.wait(SV, ("mmean", grp))
            em.do(lambda: t.matmul(banks[6][:, 0:GRP], s_onesr[:], s_mrow[:],
                                   start=True, stop=True))
            em.inc(SP, ("mb", grp), 1)
            em.wait(SA, ("rstd", grp))
            em.do(lambda: t.matmul(banks[7][:, 0:GRP], s_onesr[:], s_rrow[:],
                                   start=True, stop=True))
            em.inc(SP, ("rb", grp), 1)
            bfree[6] = (SV, ("xnr", grp, 7))
            bfree[7] = (SV, ("xnr", grp, 7))
            use_bank(5)
            for tt in range(4):
                for dc in range(8):
                    if tt == 0:
                        em.wait(SV, ("xn", grp, dc))
                    em.do(lambda tt=tt, dc=dc: t.matmul(
                        banks[5][0:128, tt * 128:tt * 128 + 8],
                        s_xnT[:, dc * GRP + tt * 128: dc * GRP + (tt + 1) * 128],
                        s_wr[:, dc * 8:(dc + 1) * 8],
                        start=(dc == 0), stop=(dc == 7)))
                em.inc(SP, ("logits", grp, tt), 1)
            bfree[5] = (SA, ("exp", grp, 3))

            for e in range(NE):
                w1buf = None
                for f in range(32):
                    fb, k = divmod(f, 4)
                    if k == 0:
                        em.wait(PW1[(grp * NE * 8 + e * 8 + fb) % 2],
                                ("w1", grp, e, fb))
                        w1buf = s_w1b[(grp * NE * 8 + e * 8 + fb + 1) % 2]
                    use_bank(f % 4)
                    for dc in range(8):
                        em.do(lambda w1buf=w1buf, f=f, dc=dc, k=k: t.matmul(
                            banks[f % 4][:],
                            w1buf[:, dc * 512 + k * 128: dc * 512 + (k + 1) * 128],
                            ch(s_xnT, dc),
                            start=(dc == 0), stop=(dc == 7)))
                    em.inc(SP, ("mm1", grp, e, f), 1)
                    bfree[f % 4] = (SA, ("gelu", grp, e, f))
                if e == 0:
                    use_bank(4)
                    for tt in range(4):
                        em.wait(SV, ("gatesb", grp, tt))
                        em.do(lambda tt=tt: t.transpose(
                            banks[4][0:128, 0:256].bitcast(BF16)[0:8, tt * 128:(tt + 1) * 128],
                            s_gatesb[:, tt * 8:(tt + 1) * 8], s_id[:]))
                        em.inc(SP, ("gtr", grp, tt), 1)
                    bfree[4] = (SA, ("gTev", grp))
                    use_bank(3)
                    for ge in range(8):
                        if ge == 0:
                            em.wait(SA, ("gTev", grp))
                        else:
                            em.wait(SA, ("gbev", grp, ge - 1))
                        em.do(lambda ge=ge: t.matmul(
                            banks[3][:, 0:GRP], s_sel[:, ge * 128:(ge + 1) * 128],
                            s_gT[:], start=True, stop=True))
                        em.inc(SP, ("gbc", grp, ge), 1)
                    bfree[3] = (SA, ("gbev", grp, 7))
                for dp in range(8):
                    idx = grp * NE * 8 + e * 8 + dp
                    em.wait(PW2[idx % 2] if grp == 0 else PS2[idx % 2],
                            ("w2", grp, e, dp))
                    w2buf = s_w2b[(idx + 1) % 2]
                    use_bank(dp)
                    for f in range(32):
                        if dp == 0:
                            if e == 0:
                                em.wait(SA, ("gelu", grp, 0, f))
                            else:
                                em.wait(SV, ("gh", grp, e, f))
                        em.do(lambda w2buf=w2buf, dp=dp, f=f, e=e: t.matmul(
                            banks[dp][:],
                            w2buf[:, f * 128:(f + 1) * 128],
                            s_gh[:, f * GRP:(f + 1) * GRP],
                            start=(f == 0),
                            stop=(f == 31 and e != NE - 1)))
                    if e == NE - 1:
                        em.do(lambda dp=dp: t.matmul(
                            banks[dp][:], s_b2[:, dp * 128:(dp + 1) * 128],
                            s_gT[:], start=False, stop=True))
                    em.inc(SP, ("mm2p", grp, e, dp), 1)
                    bfree[dp] = (SV, ("yev", grp, e, dp))

            for i in range(8):
                use_bank(i)
            for m in range(8):
                em.wait(PM[m % 2], ("ms", grp, m))
                em.wait(SV, ("xnr", grp, m))
                for n in range(8):
                    em.do(lambda m=m, n=n: t.matmul(
                        banks[n][:], s_msb[m % 2][:, n * 128:(n + 1) * 128],
                        ch(s_xnTr, m),
                        start=(m == 0), stop=(m == 7)))
                em.inc(SP, ("convm", grp, m), 1)
            for n in range(8):
                bfree[n] = (SA, ("x1", grp, n))
            use_bank(0)
            for dc in range(8):
                em.wait(SA, ("x1", grp, dc))
                em.do(lambda dc=dc: t.matmul(
                    banks[0][0:24, 0:GRP], s_wel[:, dc * 24:(dc + 1) * 24],
                    ch(s_x1, dc), start=(dc == 0), stop=(dc == 7)))
            em.inc(SP, ("lat", grp), 1)
            bfree[0] = (SV, ("bits", grp))
            use_bank(1)
            em.wait(SV, ("bits", grp))
            em.do(lambda: t.matmul(banks[1][0:12, 0:GRP], s_ht[:], s_bits[:],
                                   start=True, stop=True))
            em.inc(SP, ("syn", grp), 1)
            bfree[1] = (SV, ("synp", grp))
            use_bank(2)
            em.wait(SV, ("synp", grp))
            em.do(lambda: t.matmul(banks[2][0:1, 0:GRP], s_ones12[:],
                                   s_synp[:], start=True, stop=True))
            em.inc(SP, ("wsum", grp), 1)
            bfree[2] = (SV, ("fix", grp))
            use_bank(3)
            em.wait(SV, ("fix", grp))
            em.do(lambda: t.matmul(banks[3][0:24, 0:GRP], s_onesb[0:1, 0:24],
                                   s_fxb[:], start=True, stop=True))
            em.inc(SP, ("fixbc", grp), 1)
            bfree[3] = (SV, ("corr", grp))
            em.wait(SV, ("corr", grp))
            for dp in range(8):
                use_bank(4 + dp % 2)
                em.do(lambda dp=dp: t.matmul(
                    banks[4 + dp % 2][:], s_wle[:, dp * 128:(dp + 1) * 128],
                    s_corr[:], start=True, stop=True))
                em.inc(SP, ("smm", grp, dp), 1)
                bfree[4 + dp % 2] = (SV, ("t1", grp, dp))

    def stream_scalar(em):
        sc = em.eng
        em.wait(PC, ("consts",))
        em.do(lambda: sc.activation(s_xn00[:], s_ptmp[:], AF.Square,
                                    accum_out=s_dsc[0:1, 1:2]))
        em.inc(SA, ("xfsq",), 1)
        em.wait(SV, ("cnt",))
        em.do(lambda: sc.activation(s_cnl[0:1, 0:5], s_cnt[0:1, 0:5], AF.Ln))
        em.inc(SA, ("cntln",), 1)
        em.wait(SP, ("bcD",))
        em.do(lambda: sc.activation(s_bcD[:], banks[2][0:128, 0:4], AF.Copy))
        em.inc(SA, ("bcDev",), 1)
        em.wait(SV, ("zz",))
        em.do(lambda: sc.activation(s_sa[:], s_zz[:], AF.Sin))
        em.inc(SA, ("sinzz",), 1)
        em.wait(SV, ("ph",))
        em.do(lambda: sc.activation(s_cosp[:], s_ph[:], AF.Sin, scale=-1.0, bias=s_onesc[:, 1:2]))
        em.inc(SA, ("cosp",), 1)
        em.do(lambda: sc.activation(s_sinp[:], s_ph[:], AF.Sin))
        em.inc(SA, ("sinp",), 1)
        em.wait(SP, ("hre", 7))
        for half in range(2):
            em.do(lambda half=half: sc.activation(
                s_hreb[0:1, half * 512:(half + 1) * 512], banks[half][0:1, 0:512],
                AF.Copy, scale=1.0 / E))
            em.inc(SA, ("hrev", half), 1)

        def w2load(grp, e, pk):
            idx = grp * NE * 8 + e * 8 + pk
            g2, r2 = divmod(idx - 2, NE * 8)
            e2, pk2 = divmod(r2, 8)
            em.wait(SP, ("mm2p", g2, e2, pk2))
            w2src = d_ws2bf if e == 0 else d_w2bf[e - 1]
            em.do(lambda w2src=w2src, pk=pk, b=(idx + 1) % 2: sc.dma_start(
                s_w2b[b][:], w2src[pk]))
            em.inc(PS2[idx % 2], ("w2", grp, e, pk), 16)

        gtmp_free = {0: None, 1: None}
        for grp in range(NG):
            for dc in range(8):
                if dc == 0:
                    em.wait(PT[grp % 2], ("xTg", grp))
                if dc >= 2:
                    em.wait(SP, ("sqsum", grp, dc - 2))
                elif grp > 0:
                    em.wait(SV, ("xnr", grp - 1, 6 + dc))
                em.do(lambda grp=grp, dc=dc: sc.activation(
                    s_sq[dc % 2][:], chx(grp, dc), AF.Square))
                em.inc(SA, ("sq", grp, dc), 1)
            if grp > 0:
                w2load(grp, 0, 0)
                w2load(grp, 0, 1)
            em.wait(SV, ("var", grp))
            em.do(lambda: sc.activation(s_vrow[:], s_vrow[:], AF.Ln, bias=s_lx[0:1, 5:6]))
            em.inc(SA, ("lnv", grp), 1)
            em.wait(SA, ("lnv", grp))
            em.do(lambda: sc.activation(s_rrow[:], s_vrow[:], AF.Exp, scale=-0.5))
            em.inc(SA, ("rstd", grp), 1)
            for tt in range(4):
                em.wait(SV, ("negmax", grp, tt))
                em.do(lambda tt=tt: sc.activation(
                    s_esm[:, tt * 8:(tt + 1) * 8],
                    banks[5][0:128, tt * 128:tt * 128 + 8],
                    AF.Exp, bias=s_nmax[:, tt:tt + 1]))
                em.inc(SA, ("exp", grp, tt), 1)
            for f in range(32):
                em.wait(SP, ("mm1", grp, 0, f))
                em.do(lambda f=f: sc.activation(
                    s_gh[:, f * GRP:(f + 1) * GRP], banks[f % 4][:],
                    AF.Gelu_apprx_tanh, bias=s_bs1[:, f:f + 1]))
                em.inc(SA, ("gelu", grp, 0, f), 1)
            for tt in range(4):
                em.wait(SP, ("gtr", grp, tt))
            em.do(lambda: sc.activation(
                s_gT[:], banks[4][0:128, 0:256].bitcast(BF16)[0:8, 0:512], AF.Copy))
            em.inc(SA, ("gTev", grp), 1)
            for ge in range(8):
                em.wait(SP, ("gbc", grp, ge))
                if grp > 0:
                    em.wait(SV, ("gh", grp - 1, NE - 1, 31))
                em.do(lambda ge=ge: sc.activation(
                    ch(s_gb, ge), banks[3][:, 0:GRP], AF.Copy))
                em.inc(SA, ("gbev", grp, ge), 1)
            if grp > 0:
                for pk in range(2, 8):
                    w2load(grp, 0, pk)
            for e in range(1, NE):
                for f in range(32):
                    em.wait(SP, ("mm1", grp, e, f))
                    if gtmp_free[f % 2] is not None:
                        em.wait(SV, gtmp_free[f % 2])
                    em.do(lambda e=e, f=f: sc.activation(
                        s_gtmp[f % 2][:], banks[f % 4][:],
                        AF.Gelu_apprx_tanh,
                        bias=s_b1[:, (e - 1) * 32 + f:(e - 1) * 32 + f + 1]))
                    em.inc(SA, ("gelu", grp, e, f), 1)
                    gtmp_free[f % 2] = ("gh", grp, e, f)
                    if grp > 0 and f == 0:
                        w2load(grp, e, 0)
                        w2load(grp, e, 1)
                if grp > 0:
                    for pk in range(2, 8):
                        w2load(grp, e, pk)
            em.wait(SP, ("convm", grp, 7))
            for n in range(8):
                em.do(lambda n=n: sc.activation(ch(s_x1, n), banks[n][:], AF.Copy))
                em.inc(SA, ("x1", grp, n), 1)
            for dp in range(8):
                em.wait(SV, ("yev", grp, NE - 1, dp))
                if grp > 0 or dp >= 2:
                    pv = grp * 8 + dp - 2
                    em.wait(SV, ("prod", pv // 8, pv % 8))
                em.do(lambda dp=dp: sc.activation(
                    s_gch[dp % 2][:], ch(s_acc, dp), AF.Silu, scale=SCALE,
                    bias=s_bs2s[:, dp:dp + 1]))
                em.inc(SA, ("silu", grp, dp), 1)

    def stream_vector(em):
        v = em.eng
        kc = [0]

        def step(fn, name=None, *, after=()):
            for w in after:
                em.wait(SV, w)
            em.do(fn)
            if name is None:
                kc[0] += 1
                name = ("v", kc[0])
            em.inc(SV, name, 1)
            return name

        em.wait(PC, ("consts",))
        step(lambda: v.tensor_scalar(s_bs2s[:], s_bs2[:], SCALE, None, ALU.mult))
        l_nbel = step(lambda: v.tensor_scalar(s_nbel[:], s_bel[:], -1.0, None, ALU.mult))
        l_syn0 = step(lambda: v.memset(s_synpf[:], 0.0), ("init",))
        l_sum = step(lambda: v.reduce_sum(s_dsc[0:1, 0:1], s_ptmp[:], axis=AXL.X))
        em.wait(SA, ("xfsq",))
        l_mean = step(lambda: v.tensor_scalar(s_dsc[0:1, 2:3], s_dsc[0:1, 0:1], 1.0 / E, None, ALU.mult), after=(l_sum,))
        l_xn00 = step(lambda: v.tensor_scalar(s_xn00[:], s_ptmp[:], s_dsc[0:1, 2:3], None, ALU.subtract), ("xn00",), after=(l_mean,))
        prev = l_xn00
        for i, n in enumerate([4, 16, 64, 256, 1024]):
            l_bm = step(lambda i=i, n=n: v.reduce_sum(s_dsc[0:1, 8 + i:9 + i], s_xn00[0:1, 0:n], axis=AXL.X), after=(prev,))
            l_bm2 = step(lambda i=i: v.tensor_scalar(s_dsc[0:1, 8 + i:9 + i], s_dsc[0:1, 8 + i:9 + i], 1.0 / n, None, ALU.mult), after=(l_bm,))
            l_gt = step(lambda i=i, n=n: v.tensor_scalar(s_ptmp[0:1, 0:n], s_xn00[0:1, 0:n], s_dsc[0:1, 8 + i:9 + i], None, ALU.is_gt), after=(l_bm2,))
            l_ct = step(lambda i=i, n=n: v.reduce_sum(s_cnt[0:1, i:i + 1], s_ptmp[0:1, 0:n], axis=AXL.X), after=(l_gt,))
            prev = step(lambda i=i: v.tensor_scalar(s_cnt[0:1, i:i + 1], s_cnt[0:1, i:i + 1], 1.0, None, ALU.max), after=(l_ct,))
        em.alias(SV, ("cnt",))
        em.wait(SA, ("cntln",))
        l1 = step(lambda: v.tensor_tensor(s_ptmp[0:1, 0:5], s_cnl[0:1, 0:5], s_lx[0:1, 0:5], ALU.mult))
        l2 = step(lambda: v.reduce_sum(s_dsc[0:1, 16:17], s_ptmp[0:1, 0:5], axis=AXL.X), after=(l1,))
        l3 = step(lambda: v.reduce_sum(s_dsc[0:1, 17:18], s_cnl[0:1, 0:5], axis=AXL.X))
        l4 = step(lambda: v.tensor_scalar(s_dsc[0:1, 18:19], s_dsc[0:1, 16:17], 5.0, None, ALU.mult), after=(l2,))
        l5 = step(lambda: v.tensor_scalar(s_dsc[0:1, 19:20], s_dsc[0:1, 17:18], SLX, None, ALU.mult), after=(l3,))
        l6 = step(lambda: v.tensor_tensor(s_dsc[0:1, 20:21], s_dsc[0:1, 18:19], s_dsc[0:1, 19:20], ALU.subtract), after=(l5,))
        l7 = step(lambda: v.tensor_scalar(s_dsc[0:1, 21:22], s_dsc[0:1, 20:21], -1.0 / DEN, None, ALU.mult), after=(l6,))
        l8 = step(lambda: v.tensor_scalar(s_dsc[0:1, 21:22], s_dsc[0:1, 21:22], 1.0, None, ALU.max), after=(l7,))
        l9 = step(lambda: v.tensor_scalar(s_dsc[0:1, 21:22], s_dsc[0:1, 21:22], 2.0, None, ALU.min), after=(l8,))
        l10 = step(lambda: v.tensor_tensor(s_dsc[0:1, 22:23], s_dsc[0:1, 21:22], s_dsc[0:1, 21:22], ALU.mult), after=(l9,))
        l11 = step(lambda: v.tensor_scalar(s_dsrow[0:1, 0:1], s_dsc[0:1, 22:23], 0.2, None, ALU.mult), after=(l10,))
        l12 = step(lambda: v.tensor_tensor(s_dsrow[0:1, 0:1], s_dsrow[0:1, 0:1], s_dsc[0:1, 21:22], ALU.add), after=(l11,))
        step(lambda: v.tensor_scalar(s_dsrow[0:1, 1:2], s_dsc[0:1, 21:22], 0.5, 1.0, ALU.mult, ALU.add))
        step(lambda: v.tensor_scalar(s_dsrow[0:1, 2:3], s_dsc[0:1, 21:22], 0.1, None, ALU.mult))
        step(lambda: v.tensor_scalar(s_dsrow[0:1, 3:4], s_dsc[0:1, 21:22], 0.0, None, ALU.mult), ("dsrow",))
        em.wait(SA, ("bcDev",))
        l_ang = step(lambda: v.tensor_scalar(s_ang[:], s_ka[:], s_bcD[:, 1:2], s_bcD[:, 0:1], ALU.mult, ALU.add))
        l_ph1 = step(lambda: v.tensor_scalar(s_ph[:], s_ka2[:], s_bcD[:, 2:3], None, ALU.mult))
        l_ph2 = step(lambda: v.tensor_tensor(s_ph[:], s_ph[:], s_ka[:], ALU.subtract), after=(l_ph1,))
        step(lambda: v.tensor_scalar(s_ph[:], s_ph[:], s_bcD[:, 0:1], None, ALU.add), ("ph",), after=(l_ph2,))
        l_mgt = step(lambda: v.tensor_scalar(s_mgt[:], s_ang[:], PI, None, ALU.is_gt), after=(l_ang,))
        step(lambda: v.scalar_tensor_tensor(s_zz[:], s_mgt[:], -PI, s_ang[:], ALU.mult, ALU.add), ("zz",), after=(l_mgt,))
        l_sgn = step(lambda: v.tensor_scalar(s_sgn[:], s_mgt[:], -2.0, 1.0, ALU.mult, ALU.add))
        em.wait(SA, ("sinzz",))
        l_sing = step(lambda: v.tensor_tensor(s_sing[:], s_sa[:], s_sgn[:], ALU.mult), after=(l_sgn,))
        em.wait(SA, ("cosp",))
        l_rf = step(lambda: v.tensor_tensor(s_rf[:], s_sing[:], s_cosp[:], ALU.mult), after=(l_sing,))
        step(lambda: v.tensor_copy(s_rfb[:], s_rf[:]), ("rfb",), after=(l_rf,))
        em.wait(SA, ("sinp",))
        l_if = step(lambda: v.tensor_tensor(s_if[:], s_sing[:], s_sinp[:], ALU.mult))
        step(lambda: v.tensor_copy(s_ifb[:], s_if[:]), ("ifb",), after=(l_if,))

        for grp in range(NG):
            em.wait(SP, ("msum", grp))
            if grp > 0:
                em.wait(SA, ("silu", grp - 1, 2))
            l_mm = step(lambda: v.tensor_scalar(s_mrow[:], banks[6][0:1, 0:GRP], 1.0 / E, None, ALU.mult), ("mmean", grp))
            em.wait(SP, ("sqsum", grp, 7))
            l_vr = step(lambda: v.tensor_scalar(s_vrow[:], banks[7][0:1, 0:GRP], 1.0 / E, None, ALU.mult))
            l_pt = step(lambda: v.tensor_tensor(s_ptmp[0:1, 0:GRP], s_mrow[:], s_mrow[:], ALU.mult),
                        after=(l_mm,) if grp else (l_mm, l2))
            step(lambda: v.tensor_tensor(s_vrow[:], s_vrow[:], s_ptmp[0:1, 0:GRP], ALU.subtract), ("var", grp), after=(l_pt,))
            em.wait(SP, ("mb", grp))
            em.wait(SP, ("rb", grp))
            lt = {}
            for dc in range(8):
                lt[dc] = step(lambda grp=grp, dc=dc: v.tensor_tensor(
                    s_xtmp[dc % 2][:], chx(grp, dc), banks[6][:, 0:GRP], ALU.subtract),
                    after=(("xn", grp, dc - 2),) if dc >= 2 else ())
                if dc >= 1:
                    step(lambda dc=dc: v.tensor_tensor(
                        ch(s_xnT, dc - 1), s_xtmp[(dc - 1) % 2][:], banks[7][:, 0:GRP], ALU.mult),
                        ("xn", grp, dc - 1), after=(lt[dc - 1],))
            step(lambda: v.tensor_tensor(
                ch(s_xnT, 7), s_xtmp[1][:], banks[7][:, 0:GRP], ALU.mult),
                ("xn", grp, 7), after=(lt[7],))
            lr = {}
            for dc in range(8):
                em.wait(PR[dc % 2], ("xTr", grp, dc))
                lr[dc] = step(lambda dc=dc: v.tensor_tensor(
                    s_xtmp[dc % 2][:], s_sq[dc % 2][:], banks[6][:, 0:GRP], ALU.subtract),
                    after=(("xnr", grp, dc - 2),) if dc >= 2 else
                          (("xn", grp, 6 + dc),))
                if dc >= 1:
                    step(lambda dc=dc: v.tensor_tensor(
                        ch(s_xnTr, dc - 1), s_xtmp[(dc - 1) % 2][:], banks[7][:, 0:GRP], ALU.mult),
                        ("xnr", grp, dc - 1), after=(lr[dc - 1],))
            step(lambda: v.tensor_tensor(
                ch(s_xnTr, 7), s_xtmp[1][:], banks[7][:, 0:GRP], ALU.mult),
                ("xnr", grp, 7), after=(lr[7],))
            for tt in range(4):
                em.wait(SP, ("logits", grp, tt))
                step(lambda tt=tt: v.reduce_max(
                    s_nmax[:, tt:tt + 1], banks[5][0:128, tt * 128:tt * 128 + 8],
                    axis=AXL.X, negate=True), ("negmax", grp, tt))
            for tt in range(4):
                sl = slice(tt * 8, (tt + 1) * 8)
                em.wait(SA, ("exp", grp, tt))
                a1 = step(lambda sl=sl, tt=tt: v.reduce_sum(s_ssum[:, tt:tt + 1], s_esm[:, sl], axis=AXL.X))
                a2 = step(lambda tt=tt: v.reciprocal(s_srec[:, tt:tt + 1], s_ssum[:, tt:tt + 1]), after=(a1,))
                a3 = step(lambda sl=sl, tt=tt: v.tensor_scalar(s_probs[:, sl], s_esm[:, sl], s_srec[:, tt:tt + 1], None, ALU.mult), after=(a2,))
                a4 = step(lambda sl=sl, tt=tt: v.reduce_max(s_v1[:, tt:tt + 1], s_probs[:, sl], axis=AXL.X), after=(a3,))
                a5 = step(lambda sl=sl, tt=tt: v.tensor_scalar(s_c1[:, sl], s_probs[:, sl], s_v1[:, tt:tt + 1], None, ALU.is_lt), after=(a4,))
                a6 = step(lambda sl=sl: v.tensor_tensor(s_p2[:, sl], s_probs[:, sl], s_c1[:, sl], ALU.mult), after=(a5,))
                a7 = step(lambda sl=sl, tt=tt: v.reduce_max(s_v2[:, tt:tt + 1], s_p2[:, sl], axis=AXL.X), after=(a6,))
                a8 = step(lambda sl=sl, tt=tt: v.tensor_scalar(s_gmask[:, sl], s_probs[:, sl], s_v2[:, tt:tt + 1], None, ALU.is_ge), after=(a7,))
                a9 = step(lambda sl=sl: v.tensor_tensor(s_gates[:, sl], s_probs[:, sl], s_gmask[:, sl], ALU.mult), after=(a8,))
                step(lambda sl=sl: v.tensor_copy(s_gatesb[:, sl], s_gates[:, sl]), ("gatesb", grp, tt), after=(a9,))
            for e in range(NE):
                if e > 0:
                    em.wait(SA, ("gbev", grp, e - 1))
                    for f in range(32):
                        em.wait(SA, ("gelu", grp, e, f))
                        step(lambda e=e, f=f: v.tensor_tensor(
                            s_gh[:, f * GRP:(f + 1) * GRP], s_gtmp[f % 2][:],
                            ch(s_gb, e - 1), ALU.mult), ("gh", grp, e, f))
                if e == 0 and grp > 0:
                    em.wait(SA, ("silu", grp - 1, 7))
                for dp in range(8):
                    em.wait(SP, ("mm2p", grp, e, dp))
                    if e == 0:
                        step(lambda dp=dp: v.tensor_copy(ch(s_acc, dp), banks[dp][:]),
                             ("yev", grp, e, dp))
                    else:
                        step(lambda dp=dp: v.tensor_tensor(
                            ch(s_acc, dp), ch(s_acc, dp), banks[dp][:], ALU.add),
                            ("yev", grp, e, dp))
            em.wait(SP, ("lat", grp))
            l_bf = step(lambda: v.tensor_scalar(s_bitsf[:], banks[0][0:24, 0:GRP], s_nbel[:], None, ALU.is_gt), after=(l_nbel,))
            step(lambda: v.tensor_copy(s_bits[:], s_bitsf[:]), ("bits", grp), after=(l_bf,))
            em.wait(SP, ("syn", grp))
            l_sf = step(lambda: v.tensor_scalar(s_synpf[0:12, :], banks[1][0:12, 0:GRP], 0.5, None, ALU.is_gt), after=(l_syn0,))
            step(lambda: v.tensor_copy(s_synp[:], s_synpf[0:12, :]), ("synp", grp), after=(l_sf,))
            em.wait(SP, ("wsum", grp))
            l_f1 = step(lambda: v.tensor_scalar(s_ptmp[0:1, 0:GRP], banks[2][0:1, 0:GRP], 0.5, None, ALU.is_gt))
            l_f2 = step(lambda: v.tensor_scalar(s_ptmp[0:1, GRP:2 * GRP], banks[2][0:1, 0:GRP], 3.5, None, ALU.is_lt))
            step(lambda: v.tensor_tensor(s_fxb[:], s_ptmp[0:1, 0:GRP], s_ptmp[0:1, GRP:2 * GRP], ALU.mult), ("fix", grp), after=(l_f2,))
            em.wait(SP, ("fixbc", grp))
            l_rT = step(lambda: v.tensor_tensor(s_rT[:], s_synpf[:], banks[3][0:24, 0:GRP], ALU.mult))
            l_uT = step(lambda: v.tensor_tensor(s_uT[:], s_rT[:], s_bitsf[:], ALU.mult), after=(l_rT,))
            l_rT2 = step(lambda: v.tensor_tensor(s_rT[:], s_bitsf[:], s_rT[:], ALU.add), after=(l_uT,))
            step(lambda: v.scalar_tensor_tensor(s_corr[:], s_uT[:], -2.0, s_rT[:], ALU.mult, ALU.add), ("corr", grp), after=(l_rT2,))
            for dp in range(8):
                em.wait(SP, ("smm", grp, dp))
                l_t1 = step(lambda dp=dp: v.tensor_scalar(
                    s_t1[dp % 2][:], banks[4 + dp % 2][:], s_ble[:, dp:dp + 1],
                    None, ALU.add), ("t1", grp, dp),
                    after=(("sev", grp, dp - 2),) if dp >= 2 else ())
                step(lambda dp=dp: v.tensor_tensor(
                    ch(s_sT, dp), s_t1[dp % 2][:], ch(s_x1, dp), ALU.add),
                    ("sev", grp, dp), after=(l_t1,))
            for dp in range(8):
                em.wait(SA, ("silu", grp, dp))
                l_pr = step(lambda dp=dp: v.tensor_tensor(
                    s_pch[dp % 2][:], s_gch[dp % 2][:], ch(s_sT, dp), ALU.mult),
                    ("prod", grp, dp),
                    after=(("sev", grp, dp), ("out", grp, dp - 2))
                    if dp >= 2 else (("sev", grp, dp),))
                if grp > 0 or dp >= 2:
                    pv = grp * 8 + dp - 2
                    em.wait(SO[dp % 2], ("st", pv // 8, pv % 8))
                step(lambda grp=grp, dp=dp: v.tensor_tensor(
                    s_och[dp % 2][:], s_pch[dp % 2][:], chx(grp, dp), ALU.add),
                    ("out", grp, dp), after=(l_pr,))

    labels = {}
    streams = [
        ("sync", stream_sync), ("gpsimd", stream_gpsimd),
        ("tensor", stream_tensor), ("scalar", stream_scalar),
        ("vector", stream_vector),
    ]
    for _, fn in streams:
        fn(Em("count", labels))

    engmap = {"sync": blk.sync, "gpsimd": blk.gpsimd, "tensor": blk.tensor,
              "scalar": blk.scalar, "vector": blk.vector}
    for name, fn in streams:
        engmap[name](lambda eng, fn=fn: fn(Em("emit", labels, eng)))

    es.close()
    return nc


def _host_inputs(inputs):
    f32 = np.float32
    x = np.asarray(inputs["L0_attn_x"], dtype=f32)
    W1 = np.asarray(inputs["L0_ffn_W1"], dtype=f32)
    W2 = np.asarray(inputs["L0_ffn_W2"], dtype=f32)
    Ws1 = np.asarray(inputs["L0_ffn_Ws1"], dtype=f32)
    Ws2 = np.asarray(inputs["L0_ffn_Ws2"], dtype=f32)
    wr = np.asarray(inputs["L0_ffn_w_router"], dtype=f32)
    Wel = np.asarray(inputs["L0_attn_W_el"], dtype=f32)
    Wle = np.asarray(inputs["L0_attn_W_le"], dtype=f32)
    b1 = np.asarray(inputs["L0_ffn_b1"], dtype=f32)
    bs1 = np.asarray(inputs["L0_ffn_bs1"], dtype=f32)
    b2 = np.asarray(inputs["L0_ffn_b2"], dtype=f32)
    bs2 = np.asarray(inputs["L0_ffn_bs2"], dtype=f32)
    ble = np.asarray(inputs["L0_attn_b_le"], dtype=f32)
    bel = np.asarray(inputs["L0_attn_b_el"], dtype=f32)

    def c(a):
        return np.ascontiguousarray(a)

    W1TT = c(W1.transpose(0, 2, 1).reshape(8, 8, 128, 8, 512)
             .transpose(0, 3, 2, 1, 4).reshape(8, 8, 128, 4096))
    Ws1TT = c(Ws1.T.reshape(8, 128, 8, 512).transpose(2, 1, 0, 3)
              .reshape(8, 128, 4096))
    W2TT = c(W2.transpose(0, 2, 1).reshape(8, 32, 128, 8, 128)
             .transpose(0, 3, 2, 1, 4).reshape(8, 8, 128, 4096))
    Ws2TT = c(Ws2.T.reshape(32, 128, 8, 128).transpose(2, 1, 0, 3)
              .reshape(8, 128, 4096))
    wrT3 = c(wr.T.reshape(8, 128, 8))
    WelT3 = c(Wel.reshape(8, 128, 24))
    b1r = c(b1.reshape(8, 32, 128).transpose(0, 2, 1))
    bs1r = c(bs1.reshape(32, 128).T)
    bs2r = c(bs2.reshape(8, 128).T)
    bler = c(ble.reshape(8, 128).T)
    belr = c(bel.reshape(24, 1))

    _A = np.array([
        [1, 1, 1, 0, 1, 1, 1, 0, 0, 0, 1, 1], [1, 1, 0, 1, 1, 1, 0, 0, 0, 1, 1, 0],
        [1, 0, 1, 1, 1, 0, 0, 0, 1, 1, 0, 1], [0, 1, 1, 1, 0, 0, 0, 1, 1, 0, 1, 1],
        [0, 0, 0, 0, 1, 1, 1, 1, 1, 1, 1, 1], [1, 1, 1, 1, 0, 0, 0, 0, 0, 0, 0, 0],
        [0, 0, 0, 0, 1, 1, 1, 1, 0, 0, 0, 0], [1, 1, 0, 0, 1, 1, 0, 0, 1, 1, 0, 0],
        [1, 0, 1, 0, 1, 0, 1, 0, 1, 0, 1, 0], [0, 1, 0, 1, 0, 1, 0, 1, 0, 1, 0, 1],
        [1, 0, 0, 1, 1, 0, 0, 1, 1, 0, 0, 1], [0, 1, 1, 0, 0, 1, 1, 0, 0, 1, 1, 0]],
        dtype=f32)
    H = np.concatenate([_A.T, np.eye(12, dtype=f32)], axis=1)
    HT = c(H.T).astype(BF)

    ident = np.eye(128, dtype=f32).astype(BF)
    sel8 = np.zeros((8, 8 * 128), dtype=f32)
    for e in range(8):
        sel8[e, e * 128:(e + 1) * 128] = 1.0
    sel8 = sel8.astype(BF)

    k = np.fft.fftfreq(E).astype(np.float64)
    kav = (np.abs(k) + 1e-8).astype(f32)
    ka = c(kav.reshape(8, 128).T)
    ka2 = c((kav.astype(np.float64) ** 2).astype(f32).reshape(8, 128).T)
    lx = np.zeros((1, 8), dtype=f32)
    lx[0, :5] = np.log([2.0, 4.0, 8.0, 16.0, 32.0])
    lx[0, 5] = 1e-5

    kk = np.arange(E, dtype=np.float64)
    ang = 2.0 * np.pi * np.outer(kk, kk) / E
    Ck = np.cos(ang).astype(f32).reshape(8, 128, E).astype(BF)
    SkN = (-np.sin(ang)).astype(f32).reshape(8, 128, E).astype(BF)

    shared = {
        "W1TT": W1TT, "W2TT": W2TT, "Ws1TT": Ws1TT, "Ws2TT": Ws2TT,
        "wrT3": wrT3, "WelT3": WelT3, "Wle": c(Wle),
        "b1r": b1r, "bs1r": bs1r, "b2": c(b2), "bs2r": bs2r,
        "bler": bler, "bel": belr, "HT": HT, "ident": ident, "sel8": sel8,
        "onesf": np.stack([np.ones(128, dtype=f32), np.full(128, np.pi / 2, dtype=f32)], axis=1),
        "onesrf": np.ones((1, 128), dtype=f32),
        "onesrb": np.ones((1, 128), dtype=f32).astype(BF),
        "ones12b": np.ones((12, 1), dtype=f32).astype(BF),
        "ka": ka, "ka2": ka2, "lx": lx, "Ck": c(Ck), "SkN": c(SkN),
        "xfirst": c(x[0, 0].reshape(1, E)),
    }
    in_maps = []
    rows = B // NCORE
    for core in range(NCORE):
        xs = x[core * rows:(core + 1) * rows].reshape(TC, E)
        m = dict(shared)
        m["xT"] = c(xs.T)
        m["xTrev"] = c(xs.T[::-1])
        in_maps.append(m)
    return in_maps


def kernel(**inputs):
    in_maps = _host_inputs(inputs)
    nc = build_program()
    res = run_bass_kernel_spmd(nc, in_maps, core_ids=list(range(NCORE)))
    outs = []
    for core in range(NCORE):
        oT = res.results[core]["outT"]
        outs.append(np.ascontiguousarray(oT.T))
    return np.concatenate(outs, axis=0).reshape(B, S, E)
